# version 10
# baseline (speedup 1.0000x reference)
import numpy as np
import ml_dtypes
import concourse.bass as bass
import concourse.mybir as mybir
from concourse.bass_utils import run_bass_kernel_spmd

F32 = mybir.dt.float32
BF16 = mybir.dt.bfloat16
AF = mybir.ActivationFunctionType
ALU = mybir.AluOpType
AX = mybir.AxisListType

L = 4096
D = 1024
DI = 2048
DS = 16
DTR = 64
E = 8
T = 512
NCH = L // T
NEC = DI // 128
NH = 2
HS = DS // NH
NFC = D // 128

ENGINES = ("act", "dve", "pool", "pe", "sp")


class Evt:
    __slots__ = ("sem", "val")
    def __init__(self, sem, val):
        self.sem = sem
        self.val = val


class Trk:
    __slots__ = ("w", "r")
    def __init__(self, w=None):
        self.w = list(w) if w else []
        self.r = {}


class Sched:
    def __init__(self, nc, es):
        self.nc = nc
        self.es = es
        self.ops = {e: [] for e in ENGINES}
        self.esem = {e: es.enter_context(nc.semaphore(f"se_{e}")) for e in ENGINES}
        self.ecount = {e: 0 for e in ENGINES}
        self.owner = {id(self.esem[e]): e for e in ENGINES}
        self.qsem = {}
        self.qcount = {}

    def _op(self, eng, fn, waits, is_dma, key=None):
        if is_dma:
            if key not in self.qsem:
                self.qsem[key] = self.es.enter_context(
                    self.nc.semaphore(f"q_{key}"))
                self.qcount[key] = 0
            self.qcount[key] += 16
            evt = Evt(self.qsem[key], self.qcount[key])
        else:
            self.ecount[eng] += 1
            evt = Evt(self.esem[eng], self.ecount[eng])
        self.ops[eng].append((fn, waits, evt, is_dma))
        return evt

    def _waits(self, reads, writes, extra):
        waits = []
        for t in reads:
            waits += t.w
        for t in writes:
            waits += t.w
            waits += list(t.r.values())
        waits += list(extra)
        return waits

    def op(self, eng, fn, reads=(), writes=(), extra=()):
        evt = self._op(eng, fn, self._waits(reads, writes, extra), False)
        for t in reads:
            t.r[eng] = evt
        for t in writes:
            t.w = [evt]
            t.r = {}
        return evt

    def dma(self, eng, fn, reads=(), writes=(), extra=(), key="misc"):
        own = self.qsem.get(key)
        waits = [w for w in self._waits(reads, writes, extra)
                 if w is not None and w.sem is not own]
        evt = self._op(eng, fn, waits, True, key=key)
        for t in reads:
            t.r["q_" + key] = evt
        for t in writes:
            t.w = [evt]
            t.r = {}
        return evt

    def raw(self, eng, fn, extra=()):
        self._op(eng, fn, list(extra), None)

    def replay(self, block):
        owner = self.owner

        def run(eng_name):
            def _go(engine):
                waited = {}
                for fn, waits, evt, is_dma in self.ops[eng_name]:
                    for w in waits:
                        if w is None:
                            continue
                        if owner.get(id(w.sem)) == eng_name:
                            continue
                        key = id(w.sem)
                        if waited.get(key, -1) >= w.val:
                            continue
                        engine.wait_ge(w.sem, w.val)
                        waited[key] = w.val
                    inst = fn(engine)
                    if is_dma is None:
                        continue
                    inst.then_inc(evt.sem, 16 if is_dma else 1)
            return _go

        block.scalar(run("act"))
        block.vector(run("dve"))
        block.gpsimd(run("pool"))
        block.tensor(run("pe"))
        block.sync(run("sp"))


_DBG = False
_RACEDET = True
_LNONLY = False
_NOP0 = False
_DMASCR = 4096


def build_program():
    import contextlib
    nc = bass.Bass("TRN2", target_bir_lowering=False,
                   dynamic_dma_scratch_size=_DMASCR,
                   detect_race_conditions=_RACEDET)
    es = contextlib.ExitStack()
    S = Sched(nc, es)

    DT = nc.dram_tensor
    x_fm = DT("x_fm", [D, L], F32, kind="ExternalInput").ap()
    w_in = DT("w_in", [D, 2 * DI], F32, kind="ExternalInput").ap()
    w_out = DT("w_out", [DI, D], F32, kind="ExternalInput").ap()
    w1f = DT("w1f", [E * D, D], F32, kind="ExternalInput").ap()
    w2f = DT("w2f", [E * D, D], F32, kind="ExternalInput").ap()
    sguf = DT("sguf", [D, 2 * D], F32, kind="ExternalInput").ap()
    sdf = DT("sdf", [D, D], F32, kind="ExternalInput").ap()
    wxp_l = DT("wxp_l", [128, NEC * 96], F32, kind="ExternalInput").ap()
    wdt_f = DT("wdt_f", [128, 1024], F32, kind="ExternalInput").ap()
    rw_l = DT("rw_l", [128, NFC * E], F32, kind="ExternalInput").ap()
    gw_l = DT("gw_l", [128, NFC], F32, kind="ExternalInput").ap()
    convw_l = DT("convw_l", [128, NEC * 4], F32, kind="ExternalInput").ap()
    convb_l = DT("convb_l", [128, NEC], F32, kind="ExternalInput").ap()
    dtb_l = DT("dtb_l", [128, NEC], F32, kind="ExternalInput").ap()
    alog_l = DT("alog_l", [128, NEC * DS], F32, kind="ExternalInput").ap()
    dskip_l = DT("dskip_l", [128, NEC], F32, kind="ExternalInput").ap()
    ones_bf_d = DT("ones_bf_d", [128, 2], BF16, kind="ExternalInput").ap()
    ones1x128_d = DT("ones1x128_d", [1, 128], F32, kind="ExternalInput").ap()
    oh32_d = DT("oh32_d", [32, 32], BF16, kind="ExternalInput").ap()
    oh8_d = DT("oh8_d", [8, E], F32, kind="ExternalInput").ap()
    idn_d = DT("idn_d", [128, 128], F32, kind="ExternalInput").ap()

    out_fm = DT("out_fm", [D, L], F32, kind="ExternalOutput").ap()

    win_s = DT("win_s", [D, 2 * DI], BF16, kind="Internal").ap()
    wout_s = DT("wout_s", [DI, D], BF16, kind="Internal").ap()
    w1_s = DT("w1_s", [E * D, D], BF16, kind="Internal").ap()
    w2_s = DT("w2_s", [E * D, D], BF16, kind="Internal").ap()
    sgu_s = DT("sgu_s", [D, 2 * D], BF16, kind="Internal").ap()
    sd_s = DT("sd_s", [D, D], BF16, kind="Internal").ap()
    attn_s = DT("attn_s", [D, L], F32, kind="Internal").ap()
    stats_s = DT("stats_s", [2, L], F32, kind="Internal").ap()
    SB = nc.alloc_sbuf_tensor
    wxp_bf = SB("wxp_bf", [128, NEC * 96], BF16)
    wdt_bf = SB("wdt_bf", [128, 1024], BF16)
    rw_bf = SB("rw_bf", [128, NFC * E], BF16)
    gw_bf = SB("gw_bf", [128, NFC], BF16)
    convw = SB("convw", [128, NEC * 4], F32)
    convb = SB("convb", [128, NEC], F32)
    dtb = SB("dtb", [128, NEC], F32)
    dskip = SB("dskip", [128, NEC], F32)
    A_sb = SB("A_sb", [128, NEC * DS], F32)
    ones_bf = SB("ones_bf", [128, 2], BF16)
    ones1c = SB("ones1c", [128, 1], BF16)
    ones1x = SB("ones1x", [1, 128], F32)
    oh32 = SB("oh32", [32, 32], BF16)
    oh8 = SB("oh8", [8, E], F32)
    idn = SB("idn", [128, 128], F32)
    hist = SB("hist", [128, NEC * 3], F32)
    hcarry = SB("hcarry", [128, NEC * DS], F32)
    eps_t = SB("eps_t", [1, 1], F32)

    pa_mark = nc.sbuf_base

    rA = SB("rA", [1, T], F32)
    rB = SB("rB", [1, T], F32)
    rC = SB("rC", [1, T], F32)
    wosb = [SB(f"wosb{i}", [128, T], BF16) for i in range(2)]
    c_f32 = [SB(f"c_f32{i}", [128, T], F32) for i in range(2)]
    c_bf = [SB(f"c_bf{i}", [128, T], BF16) for i in range(2)]
    xrot = [SB(f"xrot{i}", [128, T], F32) for i in range(2)]
    xb_scr = SB("xb0", [128, T], BF16)
    x2_scr = SB("x20", [128, T], BF16)
    xn_bf = SB("xn_bf", [128, NFC * T], BF16)
    winp = [SB(f"winp{i}", [128, NFC * 128], BF16) for i in range(2)]
    xp_bf = SB("xp_bf", [128, NEC * 520], BF16)
    cv0 = SB("cv0", [128, T], F32)
    cv1 = SB("cv1", [128, T], F32)
    u_bf = SB("u_bf", [128, NEC * T], BF16)
    du_scr = SB("du0", [128, T], BF16)
    siluz = SB("siluz", [128, NEC * T], BF16)
    delta = SB("delta", [128, NEC * T], F32)
    dtr_bf = SB("dtr_bf", [128, T], BF16)
    bc_bf = SB("bc_bf", [32, T], BF16)
    B_ball = [SB(f"Bball{h}", [128, HS * T], BF16) for h in range(NH)]
    C_ball = [SB(f"Cball{h}", [128, HS * T], BF16) for h in range(NH)]
    off_dA = nc.sbuf_base
    dA_b = SB("dA_b", [128, HS * T], F32)
    off_dbu = nc.sbuf_base
    dbu_b = SB("dbu_b", [128, HS * T], BF16)
    h_b = SB("h_b", [128, HS * T], BF16)
    hC_b2 = [SB(f"hC_b{i}", [128, HS * T], BF16) for i in range(2)]
    yhalf = [SB(f"yh{h}", [128, T], F32) for h in range(NH)]
    ybf = [SB(f"ybf{i}", [128, T], BF16) for i in range(2)]
    yscr = SB("yscr", [128, T], F32)
    wo_scr = SB("wo0", [128, T], F32)
    s1 = nc.alloc_sbuf_tensor_at("s1", [128, 4 * T], F32, offset=off_dbu)
    s2 = nc.alloc_sbuf_tensor_at("s2", [128, 2 * T], F32, offset=off_dA)
    pa_used = nc.sbuf_base

    PS = nc.alloc_psum_tensor
    pp = [PS(f"pp{i}", [128, T], F32) for i in range(2)]
    pp2 = PS("pp2", [128, T], F32)
    pb = [PS(f"pb{i}", [128, T], F32) for i in range(2)]
    po = [PS(f"po{i}", [128, T], F32) for i in range(2)]
    pstat = PS("pstat", [128, T], F32)

    t_xrot = [Trk() for _ in range(2)]
    t_xb = Trk()
    t_x2 = Trk()
    t_xn = [Trk() for _ in range(NFC)]
    t_winp = [Trk() for _ in range(2)]
    t_xp = [Trk() for _ in range(NEC)]
    t_cv0, t_cv1 = Trk(), Trk()
    t_u = [Trk() for _ in range(NEC)]
    t_du = Trk()
    t_siluz = [Trk() for _ in range(NEC)]
    t_delta = [Trk() for _ in range(NEC)]
    t_dtr = Trk()
    t_bc = Trk()
    t_B = [Trk() for _ in range(NH)]
    t_C = [Trk() for _ in range(NH)]
    t_dA = [Trk() for _ in range(HS)]
    t_dbu = Trk()
    t_hh = Trk()
    t_hC = [Trk(), Trk()]
    t_yh = [Trk() for _ in range(NH)]
    t_yscr = Trk()
    t_ybf = [Trk(), Trk()]
    t_wo = Trk()
    t_hist = [Trk() for _ in range(NEC)]
    t_carry = {(ec, h): Trk() for ec in range(NEC) for h in range(NH)}
    t_pp = [Trk() for _ in range(2)]
    t_pp2 = Trk()
    t_pb = [Trk() for _ in range(2)]
    t_po = [Trk() for _ in range(2)]
    t_pstat = Trk()
    t_rows = {k: Trk() for k in ("m", "e", "q", "r", "mr")}
    t_win_s = Trk()
    t_wout_s = Trk()
    t_w1_s = Trk()
    t_w2_s = Trk()
    t_sgu_s = Trk()
    t_sd_s = Trk()
    t_attn = [Trk() for _ in range(NCH)]
    t_stats = [Trk() for _ in range(NCH)]
    t_cf = [Trk(), Trk()]
    t_cb2 = [Trk(), Trk()]
    strip_rot = [0]
    t_const = Trk()
    t_Asb = Trk()
    t_wxp, t_wdt, t_rw, t_gw = Trk(), Trk(), Trk(), Trk()
    t_wosb = [Trk() for _ in range(2)]

    S.op("dve", lambda e: e.memset(ones_bf[:], 1.0), writes=(t_const,))
    S.op("dve", lambda e: e.memset(ones1c[:], 1.0), writes=(t_const,))
    for dst, src in ((ones1x, ones1x128_d),
                     (oh32, oh32_d), (oh8, oh8_d), (idn, idn_d),
                     (convw, convw_l), (convb, convb_l), (dtb, dtb_l),
                     (dskip, dskip_l)):
        S.dma("sp", lambda e, d=dst, s=src: e.dma_start(out=d[:], in_=s),
              writes=(t_const,), key="const")
    S.dma("sp", lambda e: e.dma_start(out=A_sb[:], in_=alog_l),
          writes=(t_Asb, t_const), key="const")
    S.op("act", lambda e: e.activation(out=A_sb[:], in_=A_sb[:], func=AF.Exp),
         reads=(t_Asb,), writes=(t_Asb,))
    S.op("dve", lambda e: e.tensor_scalar(out=A_sb[:], in0=A_sb[:],
         scalar1=-1.0, scalar2=None, op0=ALU.mult),
         reads=(t_Asb,), writes=(t_Asb,))
    S.op("dve", lambda e: e.memset(hcarry[:], 0.0),
         writes=tuple(t_carry.values()))
    S.op("pool", lambda e: e.memset(hist[:], 0.0), writes=tuple(t_hist))
    S.op("pool", lambda e: e.memset(xp_bf[:], 0.0), writes=tuple(t_xp))
    S.op("dve", lambda e: e.memset(eps_t[:], 1e-5), writes=(t_const,))

    def cast_small(dst_bf, src_dram, tr_dst, cols, parts=128):
        for lo in range(0, cols, T):
            i = strip_rot[0] = 1 - strip_rot[0]
            w = min(T, cols - lo)
            S.dma("sp", lambda e, s=src_dram, lo=lo, w=w, p=parts, i=i:
                  e.dma_start(
                out=c_f32[i][0:p, 0:w], in_=s[:, lo:lo+w]), writes=(t_cf[i],),
                key=f"cf{i}")
            S.op("act", lambda e, d=dst_bf, lo=lo, w=w, p=parts, i=i: e.copy(
                out=d[0:p, lo:lo+w], in_=c_f32[i][0:p, 0:w]),
                reads=(t_cf[i],), writes=(tr_dst,))
    cast_small(wxp_bf, wxp_l, t_wxp, NEC * 96)
    cast_small(rw_bf, rw_l, t_rw, NFC * E)
    cast_small(gw_bf, gw_l, t_gw, NFC)
    cast_small(wdt_bf, wdt_f, t_wdt, 1024)

    def cast_strip(src_ap, dst_ap, tr_dst, eng, skey):
        i = strip_rot[0] = 1 - strip_rot[0]
        S.dma("act", lambda e, s=src_ap, i=i: e.dma_start(
            out=c_f32[i][:], in_=s), writes=(t_cf[i],), key=f"cf{i}")
        if eng == "act":
            S.op(eng, lambda e, i=i: e.copy(out=c_bf[i][:], in_=c_f32[i][:]),
                 reads=(t_cf[i],), writes=(t_cb2[i],))
        else:
            S.op(eng, lambda e, i=i: e.tensor_copy(out=c_bf[i][:],
                                                   in_=c_f32[i][:]),
                 reads=(t_cf[i],), writes=(t_cb2[i],))
        S.dma("act", lambda e, d=dst_ap, i=i: e.dma_start(out=d, in_=c_bf[i][:]),
              reads=(t_cb2[i],), writes=(tr_dst,), key=skey)

    for r in ([] if _NOP0 else range(0, D, 128)):
        for cc in range(0, 2 * DI, T):
            cast_strip(w_in[r:r+128, cc:cc+T], win_s[r:r+128, cc:cc+T],
                       t_win_s, "dve", "swin")
    for ec in ([] if _NOP0 else range(NEC)):
        for j in range(2):
            cast_strip(w_out[ec*128:(ec+1)*128, j*T:(j+1)*T],
                       wout_s[ec*128:(ec+1)*128, j*T:(j+1)*T], t_wout_s,
                       "dve", "swout")

    ffn_strips = []
    for eidx in range(E):
        for r in range(0, D, 128):
            rr = eidx * D + r
            for cc in range(0, D, T):
                ffn_strips.append((w1f[rr:rr+128, cc:cc+T],
                                   w1_s[rr:rr+128, cc:cc+T], t_w1_s, "sw1"))
                ffn_strips.append((w2f[rr:rr+128, cc:cc+T],
                                   w2_s[rr:rr+128, cc:cc+T], t_w2_s, "sw2"))
    for r in range(0, D, 128):
        for cc in range(0, 2 * D, T):
            ffn_strips.append((sguf[r:r+128, cc:cc+T],
                               sgu_s[r:r+128, cc:cc+T], t_sgu_s, "ssgu"))
    for r in range(0, D, 128):
        for cc in range(0, D, T):
            ffn_strips.append((sdf[r:r+128, cc:cc+T],
                               sd_s[r:r+128, cc:cc+T], t_sd_s, "ssd"))
    strips_per = (len(ffn_strips) + NCH - 1) // NCH

    def emit_prep(c):
        c0 = c * T
        for fc in range(NFC):
            i = fc % 2
            S.dma("sp", lambda e, fc=fc, i=i: e.dma_start(
                out=xrot[i][:], in_=x_fm[fc*128:(fc+1)*128, c0:c0+T]),
                writes=(t_xrot[i],), key=f"xr{i}")
            S.op("dve", lambda e, i=i: e.tensor_copy(
                out=xb_scr[:], in_=xrot[i][:]),
                reads=(t_xrot[i],), writes=(t_xb,))
            S.op("act", lambda e, i=i: e.activation(
                out=x2_scr[:], in_=xrot[i][:], func=AF.Square),
                reads=(t_xrot[i],), writes=(t_x2,))
            S.op("pe", lambda e, fc=fc: e.matmul(
                pstat[0:1, :], ones1c[:], xb_scr[:],
                start=(fc == 0), stop=(fc == NFC - 1),
                skip_group_check=True),
                reads=(t_xb, t_const), writes=(t_pstat,))
            S.op("pe", lambda e, fc=fc: e.matmul(
                pb[0][0:1, :], ones1c[:], x2_scr[:],
                start=(fc == 0), stop=(fc == NFC - 1),
                skip_group_check=True),
                reads=(t_x2, t_const), writes=(t_pb[0],))
        S.op("dve", lambda e: e.tensor_scalar(out=rA[:], in0=pstat[0:1, :],
             scalar1=1.0 / D, scalar2=None, op0=ALU.mult),
             reads=(t_pstat,), writes=(t_rows["m"],))
        S.op("dve", lambda e: e.tensor_scalar(out=rB[:], in0=pb[0][0:1, :],
             scalar1=1.0 / D, scalar2=None, op0=ALU.mult),
             reads=(t_pb[0],), writes=(t_rows["e"],))
        S.op("dve", lambda e: e.tensor_tensor(out=rC[:], in0=rA[:],
             in1=rA[:], op=ALU.mult),
             reads=(t_rows["m"],), writes=(t_rows["q"],))
        S.op("dve", lambda e: e.tensor_tensor(out=rC[:], in0=rB[:],
             in1=rC[:], op=ALU.subtract),
             reads=(t_rows["e"], t_rows["q"]), writes=(t_rows["q"],))
        S.op("act", lambda e: e.activation(out=rC[:], in_=rC[:],
             func=AF.Sqrt, bias=eps_t[:]),
             reads=(t_rows["q"], t_const), writes=(t_rows["q"],))
        S.op("dve", lambda e: e.reciprocal(out=rB[:], in_=rC[:]),
             reads=(t_rows["q"], t_rows["e"]), writes=(t_rows["r"],))
        S.op("dve", lambda e: e.tensor_tensor(out=rC[:], in0=rA[:],
             in1=rB[:], op=ALU.mult),
             reads=(t_rows["m"], t_rows["r"], t_rows["q"]),
             writes=(t_rows["mr"],))
        S.dma("act", lambda e: e.dma_start(
            out=stats_s[0:1, c0:c0+T], in_=rB[:]),
            reads=(t_rows["r"],), writes=(t_stats[c],), key="statsR")
        S.dma("act", lambda e: e.dma_start(
            out=stats_s[1:2, c0:c0+T], in_=rC[:]),
            reads=(t_rows["mr"],), writes=(t_stats[c],), key="statsM")
        S.op("pe", lambda e: e.matmul(pb[0][:], ones1x[:], rB[:],
             start=True, stop=True),
             reads=(t_rows["r"], t_const), writes=(t_pb[0],))
        S.op("pe", lambda e: e.matmul(pb[1][:], ones1x[:], rC[:],
             start=True, stop=True),
             reads=(t_rows["mr"], t_const), writes=(t_pb[1],))
        for fc in range(NFC):
            i = fc % 2
            S.dma("sp", lambda e, fc=fc, i=i: e.dma_start(
                out=xrot[i][:], in_=x_fm[fc*128:(fc+1)*128, c0:c0+T]),
                writes=(t_xrot[i],), key=f"xr{i}")
            S.op("dve", lambda e, i=i: e.tensor_tensor(
                out=xrot[i][:], in0=xrot[i][:], in1=pb[0][:], op=ALU.mult),
                reads=(t_xrot[i], t_pb[0]), writes=(t_xrot[i],))
            S.op("dve", lambda e, i=i, fc=fc: e.tensor_tensor(
                out=xn_bf[:, fc*T:(fc+1)*T], in0=xrot[i][:], in1=pb[1][:],
                op=ALU.subtract),
                reads=(t_xrot[i], t_pb[1]), writes=(t_xn[fc],))
        for foc in range(32):
            ip = foc % 2
            j = foc % 2
            lo = foc * 128
            S.dma("sp", lambda e, ip=ip, lo=lo: e.dma_start(
                out=winp[ip][:].rearrange("q (fi f) -> q fi f", fi=NFC),
                in_=win_s[:, lo:lo+128].rearrange("(fi q) f -> q fi f", q=128)),
                reads=(t_win_s,), writes=(t_winp[ip],), key=f"winp{ip}")
            for fi in range(NFC):
                S.op("pe", lambda e, ip=ip, fi=fi, j=j: e.matmul(
                    pp[j][:],
                    winp[ip][:, fi*128:(fi+1)*128],
                    xn_bf[:, fi*T:(fi+1)*T],
                    start=(fi == 0), stop=(fi == NFC - 1)),
                    reads=(t_winp[ip], t_xn[fi]), writes=(t_pp[j],))
            if foc < NEC:
                S.op("act", lambda e, foc=foc, j=j: e.copy(
                    out=xp_bf[:, foc*520+3:foc*520+515], in_=pp[j][:]),
                    reads=(t_pp[j],), writes=(t_xp[foc],))
            else:
                ec = foc - NEC
                S.op("act", lambda e, ec=ec, j=j: e.activation(
                    out=siluz[:, ec*T:(ec+1)*T], in_=pp[j][:], func=AF.Silu),
                    reads=(t_pp[j],), writes=(t_siluz[ec],))
        for ec in range(NEC):
            xo = ec * 520
            S.op("pool", lambda e, ec=ec, xo=xo: e.tensor_copy(
                out=xp_bf[:, xo:xo+3], in_=hist[:, ec*3:(ec+1)*3]),
                reads=(t_hist[ec],), writes=(t_xp[ec],))
            S.op("dve", lambda e, ec=ec, xo=xo: e.tensor_scalar(
                out=cv0[:], in0=xp_bf[:, xo:xo+T],
                scalar1=convw[:, ec*4:ec*4+1], scalar2=None, op0=ALU.mult),
                reads=(t_xp[ec], t_const), writes=(t_cv0,))
            S.op("dve", lambda e, ec=ec, xo=xo: e.scalar_tensor_tensor(
                out=cv1[:], in0=xp_bf[:, xo+1:xo+1+T],
                scalar=convw[:, ec*4+1:ec*4+2], in1=cv0[:],
                op0=ALU.mult, op1=ALU.add),
                reads=(t_xp[ec], t_const, t_cv0), writes=(t_cv1,))
            S.op("dve", lambda e, ec=ec, xo=xo: e.scalar_tensor_tensor(
                out=cv0[:], in0=xp_bf[:, xo+2:xo+2+T],
                scalar=convw[:, ec*4+2:ec*4+3], in1=cv1[:],
                op0=ALU.mult, op1=ALU.add),
                reads=(t_xp[ec], t_const, t_cv1), writes=(t_cv0,))
            S.op("dve", lambda e, ec=ec, xo=xo: e.scalar_tensor_tensor(
                out=cv1[:], in0=xp_bf[:, xo+3:xo+3+T],
                scalar=convw[:, ec*4+3:ec*4+4], in1=cv0[:],
                op0=ALU.mult, op1=ALU.add),
                reads=(t_xp[ec], t_const, t_cv0), writes=(t_cv1,))
            S.op("pool", lambda e, ec=ec, xo=xo: e.tensor_copy(
                out=hist[:, ec*3:(ec+1)*3], in_=xp_bf[:, xo+T:xo+T+3]),
                reads=(t_xp[ec],), writes=(t_hist[ec],))
            S.op("act", lambda e, ec=ec: e.activation(
                out=u_bf[:, ec*T:(ec+1)*T], in_=cv1[:], func=AF.Silu,
                bias=convb[:, ec:ec+1]),
                reads=(t_cv1, t_const), writes=(t_u[ec],))
        for ec in range(NEC):
            S.op("pe", lambda e, ec=ec: e.matmul(
                pp2[0:96, :], wxp_bf[:, ec*96:(ec+1)*96],
                u_bf[:, ec*T:(ec+1)*T],
                start=(ec == 0), stop=(ec == NEC - 1)),
                reads=(t_wxp, t_u[ec]), writes=(t_pp2,))
        S.op("act", lambda e: e.copy(out=dtr_bf[0:64, :], in_=pp2[0:64, :]),
             reads=(t_pp2,), writes=(t_dtr,))
        S.op("act", lambda e: e.copy(out=dtr_bf[64:128, :], in_=pp2[0:64, :]),
             reads=(t_pp2,), writes=(t_dtr,))
        S.op("act", lambda e: e.copy(out=bc_bf[:], in_=pp2[64:96, :]),
             reads=(t_pp2,), writes=(t_bc,))
        for ec in range(NEC):
            j = ec % 2
            if ec < 8:
                wsl = wdt_bf[0:64, ec*128:(ec+1)*128]
                rsl = dtr_bf[0:64, :]
            else:
                wsl = wdt_bf[64:128, (ec-8)*128:(ec-7)*128]
                rsl = dtr_bf[64:128, :]
            S.op("pe", lambda e, wsl=wsl, rsl=rsl, j=j: e.matmul(
                pp[j][:], wsl, rsl,
                start=True, stop=True),
                reads=(t_wdt, t_dtr), writes=(t_pp[j],))
            S.op("act", lambda e, ec=ec, j=j: e.activation(
                out=cv0[:], in_=pp[j][:], func=AF.Relu, bias=dtb[:, ec:ec+1]),
                reads=(t_pp[j], t_const), writes=(t_cv0,))
            S.op("act", lambda e, ec=ec, j=j: e.activation(
                out=cv1[:], in_=pp[j][:], func=AF.Abs, bias=dtb[:, ec:ec+1]),
                reads=(t_pp[j], t_const), writes=(t_cv1,))
            S.op("act", lambda e: e.activation(
                out=cv1[:], in_=cv1[:], func=AF.Exp, scale=-1.0),
                reads=(t_cv1,), writes=(t_cv1,))
            S.op("act", lambda e: e.activation(
                out=cv1[:], in_=cv1[:], func=AF.Ln, bias=1.0),
                reads=(t_cv1,), writes=(t_cv1,))
            S.op("dve", lambda e, ec=ec: e.tensor_tensor(
                out=delta[:, ec*T:(ec+1)*T], in0=cv0[:], in1=cv1[:],
                op=ALU.add),
                reads=(t_cv0, t_cv1), writes=(t_delta[ec],))
        for h in range(NH):
            for n in range(HS):
                gn = h * HS + n
                j = n % 2
                S.op("pe", lambda e, gn=gn, j=j: e.matmul(
                    pb[j][:], oh32[:, gn:gn+1].broadcast_to([32, 128]),
                    bc_bf[:], start=True, stop=True),
                    reads=(t_bc, t_const), writes=(t_pb[j],))
                S.op("act", lambda e, h=h, n=n, j=j: e.copy(
                    out=B_ball[h][:, n*T:(n+1)*T], in_=pb[j][:]),
                    reads=(t_pb[j],), writes=(t_B[h],))
                S.op("pe", lambda e, gn=gn, j=j: e.matmul(
                    pb[j][:], oh32[:, 16+gn:16+gn+1].broadcast_to([32, 128]),
                    bc_bf[:], start=True, stop=True),
                    reads=(t_bc, t_const), writes=(t_pb[j],))
                S.op("act", lambda e, h=h, n=n, j=j: e.copy(
                    out=C_ball[h][:, n*T:(n+1)*T], in_=pb[j][:]),
                    reads=(t_pb[j],), writes=(t_C[h],))

    def emit_scan(c):
        c0 = c * T
        banks = [pp[0], pp[1], pp2, pb[0], pb[1], po[0], po[1], pstat]
        tbanks = [t_pp[0], t_pp[1], t_pp2, t_pb[0], t_pb[1], t_po[0],
                  t_po[1], t_pstat]
        pend = []

        def emit_tree(it, ec, h):
            ib = it % 2
            S.op("dve", lambda e, ib=ib: e.tensor_tensor(
                out=s1[:], in0=hC_b2[ib][:, 0:4*T], in1=hC_b2[ib][:, 4*T:8*T],
                op=ALU.add),
                reads=(t_hC[ib],), writes=(t_dbu,))
            S.op("dve", lambda e: e.tensor_tensor(
                out=s2[:], in0=s1[:, 0:2*T], in1=s1[:, 2*T:4*T],
                op=ALU.add),
                reads=(t_dbu,), writes=(t_dA[0], t_dA[1])),
            S.op("dve", lambda e, h=h: e.tensor_tensor(
                out=yhalf[h][:], in0=s2[:, 0:T], in1=s2[:, T:2*T],
                op=ALU.add),
                reads=(t_dA[0], t_dA[1]), writes=(t_yh[h],))
            if h == NH - 1:
                yb = (it // 2) % 2
                S.op("dve", lambda e: e.tensor_tensor(
                    out=yscr[:], in0=yhalf[0][:], in1=yhalf[1][:], op=ALU.add),
                    reads=(t_yh[0], t_yh[1]), writes=(t_yscr,))
                S.op("dve", lambda e, ec=ec: e.scalar_tensor_tensor(
                    out=yscr[:], in0=u_bf[:, ec*T:(ec+1)*T],
                    scalar=dskip[:, ec:ec+1], in1=yscr[:],
                    op0=ALU.mult, op1=ALU.add),
                    reads=(t_u[ec], t_const, t_yscr), writes=(t_yscr,))
                S.op("dve", lambda e, ec=ec, yb=yb: e.tensor_tensor(
                    out=ybf[yb][:], in0=yscr[:],
                    in1=siluz[:, ec*T:(ec+1)*T], op=ALU.mult),
                    reads=(t_yscr, t_siluz[ec]), writes=(t_ybf[yb],))
                for hw in range(2):
                    ie = hw
                    S.dma("sp", lambda e, ec=ec, hw=hw, ie=ie: e.dma_start(
                        out=wosb[ie][:],
                        in_=wout_s[ec*128:(ec+1)*128, hw*T:(hw+1)*T]),
                        reads=(t_wout_s,), writes=(t_wosb[ie],), key=f"wo{ie}")
                    for dq in range(4):
                        dc = hw * 4 + dq
                        S.op("pe", lambda e, ec=ec, ie=ie, dq=dq, dc=dc,
                             yb=yb: e.matmul(
                            banks[dc][:], wosb[ie][:, dq*128:(dq+1)*128],
                            ybf[yb][:],
                            start=(ec == 0), stop=(ec == NEC - 1),
                            skip_group_check=True),
                            reads=(t_wosb[ie], t_ybf[yb]),
                            writes=(tbanks[dc],))

        it = 0
        for ec in range(NEC):
            S.op("dve", lambda e, ec=ec: e.tensor_tensor(
                out=du_scr[:], in0=delta[:, ec*T:(ec+1)*T],
                in1=u_bf[:, ec*T:(ec+1)*T], op=ALU.mult),
                reads=(t_delta[ec], t_u[ec]), writes=(t_du,))
            for h in range(NH):
                ib = it % 2
                for n in range(HS):
                    gn = h * HS + n
                    S.op("act", lambda e, ec=ec, n=n, gn=gn: e.activation(
                        out=dA_b[:, n*T:(n+1)*T],
                        in_=delta[:, ec*T:(ec+1)*T], func=AF.Exp,
                        scale=A_sb[:, ec*DS+gn:ec*DS+gn+1]),
                        reads=(t_delta[ec], t_Asb), writes=(t_dA[n],))
                S.op("dve", lambda e, h=h: e.tensor_tensor(
                    out=dbu_b[:].rearrange("p (n t) -> p n t", n=HS),
                    in0=du_scr[:, None, :].broadcast_to([128, HS, T]),
                    in1=B_ball[h][:].rearrange("p (n t) -> p n t", n=HS),
                    op=ALU.mult),
                    reads=(t_du, t_B[h]), writes=(t_dbu,))
                co = ec * DS + h * HS
                for n in range(HS):
                    S.op("dve", lambda e, n=n, co=co: e.tensor_tensor_scan(
                        out=h_b[:, n*T:(n+1)*T], data0=dA_b[:, n*T:(n+1)*T],
                        data1=dbu_b[:, n*T:(n+1)*T],
                        initial=hcarry[:, co+n:co+n+1],
                        op0=ALU.mult, op1=ALU.add),
                        reads=(t_dA[n], t_dbu, t_carry[(ec, h)]),
                        writes=(t_hh,))
                hcv = h_b[:].rearrange("p (n t) -> p n t", n=HS)[:, :, T-1]
                S.op("pool", lambda e, hcv=hcv, co=co: e.tensor_copy(
                    out=hcarry[:, co:co+HS], in_=hcv),
                    reads=(t_hh,), writes=(t_carry[(ec, h)],))
                S.op("pool", lambda e, h=h, ib=ib: e.tensor_tensor(
                    out=hC_b2[ib][:], in0=h_b[:], in1=C_ball[h][:],
                    op=ALU.mult),
                    reads=(t_hh, t_C[h]), writes=(t_hC[ib],))
                pend.append((it, ec, h))
                if len(pend) > 1:
                    emit_tree(*pend.pop(0))
                it += 1
        while pend:
            emit_tree(*pend.pop(0))
        for dc in range(NFC):
            S.op("act", lambda e, dc=dc: e.copy(
                out=wo_scr[:], in_=banks[dc][:]),
                reads=(tbanks[dc],), writes=(t_wo,))
            S.dma("act", lambda e, dc=dc, c0=c0: e.dma_start(
                out=attn_s[dc*128:(dc+1)*128, c0:c0+T], in_=wo_scr[:]),
                reads=(t_wo,), writes=(t_attn[c],), key="attn")

    strip_i = 0
    for c in range(NCH + 1):
        if c < NCH:
            emit_prep(c)
            emit_scan(c)
            hi = min(len(ffn_strips), strip_i + strips_per)
            for k in range(strip_i, hi):
                srcx, dstx, tr, skey = ffn_strips[k]
                cast_strip(srcx, dstx, tr, ("dve", "act")[k % 2], skey)
            strip_i = hi

    pa_done = []
    for eng in ENGINES:
        if S.ecount[eng]:
            pa_done.append(Evt(S.esem[eng], S.ecount[eng]))
    for k, cnt in S.qcount.items():
        if cnt:
            pa_done.append(Evt(S.qsem[k], cnt))

    pb_ptr = [((pa_mark + 63) // 64) * 64]

    def PBA(name, shape, dt):
        nbytes = int(np.prod(shape[1:])) * (2 if dt == BF16 else 4)
        off = pb_ptr[0]
        hnd = nc.alloc_sbuf_tensor_at(name, shape, dt, offset=off)
        pb_ptr[0] = ((off + nbytes + 63) // 64) * 64
        return hnd

    xchB = PBA("xchB", [128, NFC * T], F32)
    xnB = PBA("xnB", [128, NFC * T], BF16)
    gsB = PBA("gsB", [128, NFC * T], BF16)
    vsc = PBA("vsc", [128, T], F32)
    vgB = PBA("vgB", [128, NFC * T], BF16)
    hB = PBA("hB", [128, NFC * T], BF16)
    hsB = PBA("hsB", [128, NFC * T], BF16)
    oacc = PBA("oacc", [128, NFC * T], F32)
    lg_sb = PBA("lg_sb", [8, T], F32)
    lgT = PBA("lgT", [128, 32], F32)
    m1s = PBA("m1s", [128, 4], F32)
    nm1 = PBA("nm1", [128, 4], F32)
    ssum = PBA("ssum", [128, 4], F32)
    rsum = PBA("rsum", [128, 4], F32)
    m2s = PBA("m2s", [128, 4], F32)
    pexp = PBA("pexp", [128, 32], F32)
    mk1 = PBA("mk1", [128, 32], F32)
    lg2 = PBA("lg2", [128, 32], F32)
    mk2 = PBA("mk2", [128, 32], F32)
    combT = PBA("combT", [128, 32], F32)
    comb_fm = PBA("comb_fm", [8, T], F32)
    g_row = PBA("g_row", [1, T], F32)
    rB_row = PBA("rB_row", [1, T], F32)
    mrB_row = PBA("mrB_row", [1, T], F32)
    w1p = [PBA(f"w1p{i}", [128, NFC * D], BF16) for i in range(2)]
    w2p = [PBA(f"w2p{i}", [128, NFC * D], BF16) for i in range(2)]
    sguB = PBA("sguB", [128, NFC * 2 * D], BF16)
    sdB = PBA("sdB", [128, NFC * D], BF16)
    assert pb_ptr[0] <= nc.SBUF_PARTITION_SIZE_BYTES, pb_ptr[0]
    assert pb_ptr[0] <= pa_used + 64, (pb_ptr[0], pa_used)

    def TB():
        return Trk(pa_done)
    t_xchB = TB()
    t_xnB = [TB() for _ in range(NFC)]
    t_gsB = [TB() for _ in range(NFC)]
    t_vsc = TB()
    t_vgB = [TB() for _ in range(NFC)]
    t_hB = [TB() for _ in range(NFC)]
    t_hsB = [TB() for _ in range(NFC)]
    t_oacc = [TB() for _ in range(NFC)]
    t_lg = TB()
    t_lgT = TB()
    t_small = TB()
    t_combT = TB()
    t_cfm = TB()
    t_grow = TB()
    t_w1p = [TB() for _ in range(2)]
    t_w2p = [TB() for _ in range(2)]
    t_sguB = TB()
    t_sdB = TB()
    t_rB = TB()
    t_mrB = TB()
    for t in (t_pp[0], t_pp[1], t_pp2, t_pb[0], t_pb[1], t_po[0], t_po[1],
              t_pstat):
        t.w = t.w + pa_done

    out_store_evts = []

    def emit_phaseB(c):
        c0 = c * T
        for fc in range(NFC):
            S.dma("sp", lambda e, fc=fc: e.dma_start(
                out=xchB[:, fc*T:(fc+1)*T],
                in_=x_fm[fc*128:(fc+1)*128, c0:c0+T]),
                writes=(t_xchB,), key="xB")
        S.dma("sp", lambda e: e.dma_start(
            out=rB_row[:], in_=stats_s[0:1, c0:c0+T]),
            reads=(t_stats[c],), writes=(t_rB,), key="strB")
        S.dma("sp", lambda e: e.dma_start(
            out=mrB_row[:], in_=stats_s[1:2, c0:c0+T]),
            reads=(t_stats[c],), writes=(t_mrB,), key="strM")
        S.op("pe", lambda e: e.matmul(pb[0][:], ones1x[:], rB_row[:],
             start=True, stop=True),
             reads=(t_rB, t_const), writes=(t_pb[0],))
        S.op("pe", lambda e: e.matmul(pb[1][:], ones1x[:], mrB_row[:],
             start=True, stop=True),
             reads=(t_mrB, t_const), writes=(t_pb[1],))
        for fc in range(NFC):
            S.op("dve", lambda e, fc=fc: e.tensor_tensor(
                out=vsc[:], in0=xchB[:, fc*T:(fc+1)*T], in1=pb[0][:],
                op=ALU.mult),
                reads=(t_xchB, t_pb[0]), writes=(t_vsc,))
            S.op("dve", lambda e, fc=fc: e.tensor_tensor(
                out=xnB[:, fc*T:(fc+1)*T], in0=vsc[:], in1=pb[1][:],
                op=ALU.subtract),
                reads=(t_vsc, t_pb[1]), writes=(t_xnB[fc],))
        oacc_evts = []
        for fc in range(NFC):
            ev = S.dma("sp", lambda e, fc=fc: e.dma_start(
                out=oacc[:, fc*T:(fc+1)*T],
                in_=attn_s[fc*128:(fc+1)*128, c0:c0+T]),
                reads=(t_attn[c],), writes=(t_oacc[fc],), key="oaccl")
            oacc_evts.append(ev)
        for fc in range(NFC):
            t_oacc[fc].w = [oacc_evts[-1]]
        for fi in range(NFC):
            S.op("pe", lambda e, fi=fi: e.matmul(
                pp2[0:E, :], rw_bf[:, fi*E:(fi+1)*E], xnB[:, fi*T:(fi+1)*T],
                start=(fi == 0), stop=(fi == NFC - 1)),
                reads=(t_rw, t_xnB[fi]), writes=(t_pp2,))
        S.op("act", lambda e: e.copy(out=lg_sb[:], in_=pp2[0:E, :]),
             reads=(t_pp2,), writes=(t_lg,))
        for jj in range(4):
            S.op("pe", lambda e, jj=jj: e.transpose(
                pb[0][:, jj*8:(jj+1)*8], lg_sb[:, jj*128:(jj+1)*128],
                idn[0:8, 0:8]),
                reads=(t_lg, t_const), writes=(t_pb[0],))
        S.op("dve", lambda e: e.tensor_copy(out=lgT[:], in_=pb[0][:, 0:32]),
             reads=(t_pb[0],), writes=(t_lgT,))
        SL = [slice(jj*8, (jj+1)*8) for jj in range(4)]
        for jj in range(4):
            S.op("dve", lambda e, jj=jj: e.reduce_max(
                out=m1s[:, jj:jj+1], in_=lgT[:, SL[jj]], axis=AX.X),
                reads=(t_lgT,), writes=(t_small,))
        for jj in range(4):
            S.op("dve", lambda e, jj=jj: e.tensor_scalar(
                out=nm1[:, jj:jj+1], in0=m1s[:, jj:jj+1], scalar1=-1.0,
                scalar2=None, op0=ALU.mult),
                reads=(t_small,), writes=(t_small,))
        for jj in range(4):
            S.op("act", lambda e, jj=jj: e.activation(
                out=pexp[:, SL[jj]], in_=lgT[:, SL[jj]], func=AF.Exp,
                bias=nm1[:, jj:jj+1]),
                reads=(t_lgT, t_small), writes=(t_small,))
        for jj in range(4):
            S.op("dve", lambda e, jj=jj: e.reduce_sum(
                out=ssum[:, jj:jj+1], in_=pexp[:, SL[jj]], axis=AX.X),
                reads=(t_small,), writes=(t_small,))
        for jj in range(4):
            S.op("dve", lambda e, jj=jj: e.reciprocal(
                out=rsum[:, jj:jj+1], in_=ssum[:, jj:jj+1]),
                reads=(t_small,), writes=(t_small,))
        for jj in range(4):
            S.op("dve", lambda e, jj=jj: e.tensor_scalar(
                out=mk1[:, SL[jj]], in0=lgT[:, SL[jj]],
                scalar1=m1s[:, jj:jj+1], scalar2=None, op0=ALU.is_equal),
                reads=(t_lgT, t_small), writes=(t_small,))
        for jj in range(4):
            S.op("dve", lambda e, jj=jj: e.tensor_scalar(
                out=lg2[:, SL[jj]], in0=mk1[:, SL[jj]], scalar1=-1e9,
                scalar2=None, op0=ALU.mult),
                reads=(t_small,), writes=(t_small,))
        for jj in range(4):
            S.op("dve", lambda e, jj=jj: e.tensor_tensor(
                out=lg2[:, SL[jj]], in0=lgT[:, SL[jj]], in1=lg2[:, SL[jj]],
                op=ALU.add),
                reads=(t_lgT, t_small), writes=(t_small,))
        for jj in range(4):
            S.op("dve", lambda e, jj=jj: e.reduce_max(
                out=m2s[:, jj:jj+1], in_=lg2[:, SL[jj]], axis=AX.X),
                reads=(t_small,), writes=(t_small,))
        for jj in range(4):
            S.op("dve", lambda e, jj=jj: e.tensor_scalar(
                out=mk2[:, SL[jj]], in0=lg2[:, SL[jj]],
                scalar1=m2s[:, jj:jj+1], scalar2=None, op0=ALU.is_equal),
                reads=(t_small,), writes=(t_small,))
        for jj in range(4):
            S.op("dve", lambda e, jj=jj: e.tensor_tensor(
                out=mk1[:, SL[jj]], in0=mk1[:, SL[jj]], in1=mk2[:, SL[jj]],
                op=ALU.add),
                reads=(t_small,), writes=(t_small,))
        for jj in range(4):
            S.op("dve", lambda e, jj=jj: e.tensor_tensor(
                out=pexp[:, SL[jj]], in0=pexp[:, SL[jj]], in1=mk1[:, SL[jj]],
                op=ALU.mult),
                reads=(t_small,), writes=(t_small,))
        for jj in range(4):
            S.op("dve", lambda e, jj=jj: e.tensor_scalar(
                out=combT[:, SL[jj]], in0=pexp[:, SL[jj]],
                scalar1=rsum[:, jj:jj+1], scalar2=None, op0=ALU.mult),
                reads=(t_small,), writes=(t_combT,))
        for jj in range(4):
            S.op("pe", lambda e, jj=jj: e.transpose(
                pp2[0:8, jj*128:(jj+1)*128], combT[:, jj*8:(jj+1)*8], idn[:]),
                reads=(t_combT, t_const), writes=(t_pp2,))
        S.op("act", lambda e: e.copy(out=comb_fm[:], in_=pp2[0:8, :]),
             reads=(t_pp2,), writes=(t_cfm,))
        for fi in range(NFC):
            S.op("pe", lambda e, fi=fi: e.matmul(
                pstat[0:1, :], gw_bf[:, fi:fi+1], xnB[:, fi*T:(fi+1)*T],
                start=(fi == 0), stop=(fi == NFC - 1)),
                reads=(t_gw, t_xnB[fi]), writes=(t_pstat,))
        S.op("act", lambda e: e.activation(out=g_row[:], in_=pstat[0:1, :],
             func=AF.Sigmoid),
             reads=(t_pstat,), writes=(t_grow,))
        S.op("pe", lambda e: e.matmul(pb[0][:], ones1x[:], g_row[:],
             start=True, stop=True),
             reads=(t_grow, t_const), writes=(t_pb[0],))
        S.dma("sp", lambda e: e.dma_start(
            out=sguB[:].rearrange("q (fi f) -> q fi f", fi=NFC),
            in_=sgu_s[:, :].rearrange("(fi q) f -> q fi f", q=128)),
            reads=(t_sgu_s,), writes=(t_sguB,), key="sguL")
        for foc in range(16):
            j = foc % 2
            for fi in range(NFC):
                S.op("pe", lambda e, foc=foc, fi=fi, j=j: e.matmul(
                    pp[j][:], sguB[:, fi*2*D + foc*128: fi*2*D + foc*128 + 128],
                    xnB[:, fi*T:(fi+1)*T],
                    start=(fi == 0), stop=(fi == NFC - 1)),
                    reads=(t_sguB, t_xnB[fi]), writes=(t_pp[j],))
            if foc < NFC:
                S.op("act", lambda e, foc=foc, j=j: e.activation(
                    out=gsB[:, foc*T:(foc+1)*T], in_=pp[j][:], func=AF.Silu),
                    reads=(t_pp[j],), writes=(t_gsB[foc],))
            else:
                uc = foc - NFC
                S.op("dve", lambda e, uc=uc, j=j: e.tensor_tensor(
                    out=vsc[:], in0=gsB[:, uc*T:(uc+1)*T], in1=pp[j][:],
                    op=ALU.mult),
                    reads=(t_gsB[uc], t_pp[j]), writes=(t_vsc,))
                S.op("dve", lambda e, uc=uc: e.tensor_tensor(
                    out=vgB[:, uc*T:(uc+1)*T], in0=vsc[:], in1=pb[0][:],
                    op=ALU.mult),
                    reads=(t_vsc, t_pb[0]), writes=(t_vgB[uc],))
        S.dma("sp", lambda e: e.dma_start(
            out=sdB[:].rearrange("q (fi f) -> q fi f", fi=NFC),
            in_=sd_s[:, :].rearrange("(fi q) f -> q fi f", q=128)),
            reads=(t_sd_s,), writes=(t_sdB,), key="sdL")
        for dc in range(NFC):
            j = dc % 2
            for fi in range(NFC):
                S.op("pe", lambda e, dc=dc, fi=fi, j=j: e.matmul(
                    po[j][:], sdB[:, fi*D + dc*128: fi*D + dc*128 + 128],
                    vgB[:, fi*T:(fi+1)*T],
                    start=(fi == 0), stop=(fi == NFC - 1)),
                    reads=(t_sdB, t_vgB[fi]), writes=(t_po[j],))
            S.op("dve", lambda e, dc=dc, j=j: e.tensor_tensor(
                out=oacc[:, dc*T:(dc+1)*T], in0=oacc[:, dc*T:(dc+1)*T],
                in1=po[j][:], op=ALU.add),
                reads=(t_po[j], t_oacc[dc]), writes=(t_oacc[dc],))
        for eidx in range(E):
            ie = eidx % 2
            S.dma("sp", lambda e, ie=ie, eidx=eidx: e.dma_start(
                out=w1p[ie][:].rearrange("q (fi f) -> q fi f", fi=NFC),
                in_=w1_s[eidx*D:(eidx+1)*D, :].rearrange(
                    "(fi q) f -> q fi f", q=128)),
                reads=(t_w1_s,), writes=(t_w1p[ie],), key=f"w1L{ie}")
            S.dma("sp", lambda e, ie=ie, eidx=eidx: e.dma_start(
                out=w2p[ie][:].rearrange("q (fi f) -> q fi f", fi=NFC),
                in_=w2_s[eidx*D:(eidx+1)*D, :].rearrange(
                    "(fi q) f -> q fi f", q=128)),
                reads=(t_w2_s,), writes=(t_w2p[ie],), key=f"w2L{ie}")
            for foc in range(NFC):
                j = foc % 2
                for fi in range(NFC):
                    S.op("pe", lambda e, ie=ie, foc=foc, fi=fi, j=j: e.matmul(
                        pp[j][:],
                        w1p[ie][:, fi*D + foc*128: fi*D + foc*128 + 128],
                        xnB[:, fi*T:(fi+1)*T],
                        start=(fi == 0), stop=(fi == NFC - 1)),
                        reads=(t_w1p[ie], t_xnB[fi]), writes=(t_pp[j],))
                S.op("act", lambda e, foc=foc, j=j: e.activation(
                    out=hB[:, foc*T:(foc+1)*T], in_=pp[j][:], func=AF.Silu),
                    reads=(t_pp[j],), writes=(t_hB[foc],))
            S.op("pe", lambda e, eidx=eidx: e.matmul(
                pb[1][:], oh8[:, eidx:eidx+1].broadcast_to([8, 128]),
                comb_fm[:], start=True, stop=True),
                reads=(t_cfm, t_const), writes=(t_pb[1],))
            for foc in range(NFC):
                S.op("dve", lambda e, foc=foc: e.tensor_tensor(
                    out=hsB[:, foc*T:(foc+1)*T], in0=hB[:, foc*T:(foc+1)*T],
                    in1=pb[1][:], op=ALU.mult),
                    reads=(t_hB[foc], t_pb[1]), writes=(t_hsB[foc],))
            for dc in range(NFC):
                j = dc % 2
                for fi in range(NFC):
                    S.op("pe", lambda e, ie=ie, dc=dc, fi=fi, j=j: e.matmul(
                        po[j][:],
                        w2p[ie][:, fi*D + dc*128: fi*D + dc*128 + 128],
                        hsB[:, fi*T:(fi+1)*T],
                        start=(fi == 0), stop=(fi == NFC - 1)),
                        reads=(t_w2p[ie], t_hsB[fi]), writes=(t_po[j],))
                S.op("dve", lambda e, dc=dc, j=j: e.tensor_tensor(
                    out=oacc[:, dc*T:(dc+1)*T], in0=oacc[:, dc*T:(dc+1)*T],
                    in1=po[j][:], op=ALU.add),
                    reads=(t_po[j], t_oacc[dc]), writes=(t_oacc[dc],))
        st_evts = []
        for fc in range(NFC):
            S.op("dve", lambda e, fc=fc: e.tensor_tensor(
                out=oacc[:, fc*T:(fc+1)*T], in0=oacc[:, fc*T:(fc+1)*T],
                in1=xchB[:, fc*T:(fc+1)*T], op=ALU.add),
                reads=(t_xchB, t_oacc[fc]), writes=(t_oacc[fc],))
            evt = S.dma("act", lambda e, fc=fc: e.dma_start(
                out=out_fm[fc*128:(fc+1)*128, c0:c0+T],
                in_=oacc[:, fc*T:(fc+1)*T]),
                reads=(t_oacc[fc],), key="outst")
            st_evts.append(evt)
            out_store_evts.append(evt)
        for fc in range(NFC):
            t_oacc[fc].r["q_outst"] = st_evts[-1]

    if not _LNONLY:
        for c in range(NCH):
            emit_phaseB(c)

    last = out_store_evts[-1]
    S.raw("sp", lambda e, lv=last: e.wait_ge(lv.sem, lv.val))

    with nc.Block() as block:
        S.replay(block)
    return nc


_NC_CACHE = {}


def kernel(L0_attn_x, L0_attn_W_in, L0_attn_conv_w, L0_attn_conv_b,
           L0_attn_W_xp, L0_attn_W_dt, L0_attn_dt_bias, L0_attn_A_log,
           L0_attn_D_skip, L0_attn_W_out, L0_ffn_router_w, L0_ffn_w1,
           L0_ffn_w2, L0_ffn_shared_gate_up, L0_ffn_shared_down,
           L0_ffn_shared_gate_w):
    f32 = np.float32
    x = np.asarray(L0_attn_x, f32)
    ncores = x.shape[0]

    def lay_pc(v, nec):
        return np.ascontiguousarray(np.asarray(v, f32).reshape(nec, 128).T)

    w_in = np.ascontiguousarray(np.asarray(L0_attn_W_in, f32))
    w_out = np.ascontiguousarray(np.asarray(L0_attn_W_out, f32))
    w1f = np.ascontiguousarray(np.asarray(L0_ffn_w1, f32).reshape(E * D, D))
    w2f = np.ascontiguousarray(np.asarray(L0_ffn_w2, f32).reshape(E * D, D))
    sguf = np.ascontiguousarray(np.asarray(L0_ffn_shared_gate_up, f32))
    sdf = np.ascontiguousarray(np.asarray(L0_ffn_shared_down, f32))
    wxp_l = np.ascontiguousarray(
        np.asarray(L0_attn_W_xp, f32).reshape(NEC, 128, 96)
        .transpose(1, 0, 2).reshape(128, NEC * 96))
    wdt0 = np.asarray(L0_attn_W_dt, f32)
    wdt_f = np.ascontiguousarray(
        np.concatenate([wdt0[:, 0:1024], wdt0[:, 1024:2048]], axis=0))
    rw_l = np.ascontiguousarray(
        np.asarray(L0_ffn_router_w, f32).reshape(NFC, 128, E)
        .transpose(1, 0, 2).reshape(128, NFC * E))
    gw_l = np.ascontiguousarray(
        np.asarray(L0_ffn_shared_gate_w, f32).reshape(NFC, 128).T)
    convw_l = np.ascontiguousarray(
        np.asarray(L0_attn_conv_w, f32).reshape(NEC, 128, 4)
        .transpose(1, 0, 2).reshape(128, NEC * 4))
    convb_l = lay_pc(L0_attn_conv_b, NEC)
    dtb_l = lay_pc(L0_attn_dt_bias, NEC)
    alog_l = np.ascontiguousarray(
        np.asarray(L0_attn_A_log, f32).reshape(NEC, 128, DS)
        .transpose(1, 0, 2).reshape(128, NEC * DS))
    dskip_l = lay_pc(L0_attn_D_skip, NEC)

    bf = ml_dtypes.bfloat16
    ones_bf_d = np.ones((128, 2), dtype=bf)
    ones1x128_d = np.ones((1, 128), dtype=f32)
    oh32 = np.eye(32, dtype=bf)
    oh8 = np.eye(8, dtype=f32)
    idn_d = np.eye(128, dtype=f32)

    key = "p"
    if key not in _NC_CACHE:
        _NC_CACHE[key] = build_program()
    nc = _NC_CACHE[key]

    shared = dict(
        w_in=w_in, w_out=w_out, w1f=w1f, w2f=w2f, sguf=sguf, sdf=sdf,
        wxp_l=wxp_l, wdt_f=wdt_f, rw_l=rw_l, gw_l=gw_l, convw_l=convw_l,
        convb_l=convb_l, dtb_l=dtb_l, alog_l=alog_l, dskip_l=dskip_l,
        ones_bf_d=ones_bf_d, ones1x128_d=ones1x128_d, oh32_d=oh32,
        oh8_d=oh8, idn_d=idn_d)
    in_maps = []
    for i in range(ncores):
        m = dict(shared)
        m["x_fm"] = np.ascontiguousarray(x[i].T)
        in_maps.append(m)

    res = run_bass_kernel_spmd(nc, in_maps, core_ids=list(range(ncores)))
    out = np.stack([np.ascontiguousarray(r["out_fm"].T)
                    for r in res.results], axis=0)
    return out.astype(np.float32)


# version 11
# speedup vs baseline: 1.0696x; 1.0696x over previous
import numpy as np
import ml_dtypes
import concourse.bass as bass
import concourse.mybir as mybir
from concourse.bass_utils import run_bass_kernel_spmd

F32 = mybir.dt.float32
BF16 = mybir.dt.bfloat16
AF = mybir.ActivationFunctionType
ALU = mybir.AluOpType
AX = mybir.AxisListType

L = 4096
D = 1024
DI = 2048
DS = 16
DTR = 64
E = 8
T = 512
NCH = L // T
NEC = DI // 128
NH = 2
HS = DS // NH
NFC = D // 128

ENGINES = ("act", "dve", "pool", "pe", "sp")


class Evt:
    __slots__ = ("sem", "val")
    def __init__(self, sem, val):
        self.sem = sem
        self.val = val


class Trk:
    __slots__ = ("w", "r")
    def __init__(self, w=None):
        self.w = list(w) if w else []
        self.r = {}


class Sched:
    def __init__(self, nc, es):
        self.nc = nc
        self.es = es
        self.ops = {e: [] for e in ENGINES}
        self.esem = {e: es.enter_context(nc.semaphore(f"se_{e}")) for e in ENGINES}
        self.ecount = {e: 0 for e in ENGINES}
        self.owner = {id(self.esem[e]): e for e in ENGINES}
        self.qsem = {}
        self.qcount = {}

    def _op(self, eng, fn, waits, is_dma, key=None):
        if is_dma:
            if key not in self.qsem:
                self.qsem[key] = self.es.enter_context(
                    self.nc.semaphore(f"q_{key}"))
                self.qcount[key] = 0
            self.qcount[key] += 16
            evt = Evt(self.qsem[key], self.qcount[key])
        else:
            self.ecount[eng] += 1
            evt = Evt(self.esem[eng], self.ecount[eng])
        self.ops[eng].append((fn, waits, evt, is_dma))
        return evt

    def _waits(self, reads, writes, extra):
        waits = []
        for t in reads:
            waits += t.w
        for t in writes:
            waits += t.w
            waits += list(t.r.values())
        waits += list(extra)
        return waits

    def op(self, eng, fn, reads=(), writes=(), extra=()):
        evt = self._op(eng, fn, self._waits(reads, writes, extra), False)
        for t in reads:
            t.r[eng] = evt
        for t in writes:
            t.w = [evt]
            t.r = {}
        return evt

    def dma(self, eng, fn, reads=(), writes=(), extra=(), key="misc"):
        own = self.qsem.get(key)
        waits = [w for w in self._waits(reads, writes, extra)
                 if w is not None and w.sem is not own]
        evt = self._op(eng, fn, waits, True, key=key)
        for t in reads:
            t.r["q_" + key] = evt
        for t in writes:
            t.w = [evt]
            t.r = {}
        return evt

    def raw(self, eng, fn, extra=()):
        self._op(eng, fn, list(extra), None)

    def replay(self, block):
        owner = self.owner

        def run(eng_name):
            def _go(engine):
                waited = {}
                for fn, waits, evt, is_dma in self.ops[eng_name]:
                    for w in waits:
                        if w is None:
                            continue
                        if owner.get(id(w.sem)) == eng_name:
                            continue
                        key = id(w.sem)
                        if waited.get(key, -1) >= w.val:
                            continue
                        engine.wait_ge(w.sem, w.val)
                        waited[key] = w.val
                    inst = fn(engine)
                    if is_dma is None:
                        continue
                    inst.then_inc(evt.sem, 16 if is_dma else 1)
            return _go

        block.scalar(run("act"))
        block.vector(run("dve"))
        block.gpsimd(run("pool"))
        block.tensor(run("pe"))
        block.sync(run("sp"))


_DBG = False
_RACEDET = True
_LNONLY = False
_NOP0 = False
_DMASCR = 4096


def build_program():
    import contextlib
    nc = bass.Bass("TRN2", target_bir_lowering=False,
                   dynamic_dma_scratch_size=_DMASCR,
                   detect_race_conditions=_RACEDET)
    es = contextlib.ExitStack()
    S = Sched(nc, es)

    DT = nc.dram_tensor
    x_fm = DT("x_fm", [D, L], F32, kind="ExternalInput").ap()
    w_in = DT("w_in", [D, 2 * DI], F32, kind="ExternalInput").ap()
    w_out = DT("w_out", [DI, D], F32, kind="ExternalInput").ap()
    w1f = DT("w1f", [E * D, D], F32, kind="ExternalInput").ap()
    w2f = DT("w2f", [E * D, D], F32, kind="ExternalInput").ap()
    sguf = DT("sguf", [D, 2 * D], F32, kind="ExternalInput").ap()
    sdf = DT("sdf", [D, D], F32, kind="ExternalInput").ap()
    wxp_l = DT("wxp_l", [128, NEC * 96], F32, kind="ExternalInput").ap()
    wdt_f = DT("wdt_f", [128, 1024], F32, kind="ExternalInput").ap()
    rw_l = DT("rw_l", [128, NFC * E], F32, kind="ExternalInput").ap()
    gw_l = DT("gw_l", [128, NFC], F32, kind="ExternalInput").ap()
    convw_l = DT("convw_l", [128, NEC * 4], F32, kind="ExternalInput").ap()
    convb_l = DT("convb_l", [128, NEC], F32, kind="ExternalInput").ap()
    dtb_l = DT("dtb_l", [128, NEC], F32, kind="ExternalInput").ap()
    alog_l = DT("alog_l", [128, NEC * DS], F32, kind="ExternalInput").ap()
    dskip_l = DT("dskip_l", [128, NEC], F32, kind="ExternalInput").ap()
    ones_bf_d = DT("ones_bf_d", [128, 2], BF16, kind="ExternalInput").ap()
    ones1x128_d = DT("ones1x128_d", [1, 128], F32, kind="ExternalInput").ap()
    oh32_d = DT("oh32_d", [32, 32], BF16, kind="ExternalInput").ap()
    oh8_d = DT("oh8_d", [8, E], F32, kind="ExternalInput").ap()
    idn_d = DT("idn_d", [128, 128], F32, kind="ExternalInput").ap()

    out_fm = DT("out_fm", [D, L], F32, kind="ExternalOutput").ap()

    win_s = DT("win_s", [D, 2 * DI], BF16, kind="Internal").ap()
    wout_s = DT("wout_s", [DI, D], BF16, kind="Internal").ap()
    w1_s = DT("w1_s", [E * D, D], BF16, kind="Internal").ap()
    w2_s = DT("w2_s", [E * D, D], BF16, kind="Internal").ap()
    sgu_s = DT("sgu_s", [D, 2 * D], BF16, kind="Internal").ap()
    sd_s = DT("sd_s", [D, D], BF16, kind="Internal").ap()
    attn_s = DT("attn_s", [D, L], F32, kind="Internal").ap()
    stats_s = DT("stats_s", [2, L], F32, kind="Internal").ap()
    SB = nc.alloc_sbuf_tensor
    wxp_bf = SB("wxp_bf", [128, NEC * 96], BF16)
    wdt_bf = SB("wdt_bf", [128, 1024], BF16)
    rw_bf = SB("rw_bf", [128, NFC * E], BF16)
    gw_bf = SB("gw_bf", [128, NFC], BF16)
    convw = SB("convw", [128, NEC * 4], F32)
    convb = SB("convb", [128, NEC], F32)
    dtb = SB("dtb", [128, NEC], F32)
    dskip = SB("dskip", [128, NEC], F32)
    A_sb = SB("A_sb", [128, NEC * DS], F32)
    ones_bf = SB("ones_bf", [128, 2], BF16)
    ones1c = SB("ones1c", [128, 1], BF16)
    ones1x = SB("ones1x", [1, 128], F32)
    oh32 = SB("oh32", [32, 32], BF16)
    oh8 = SB("oh8", [8, E], F32)
    idn = SB("idn", [128, 128], F32)
    hist = SB("hist", [128, NEC * 3], F32)
    hcarry = SB("hcarry", [128, NEC * DS], F32)
    eps_t = SB("eps_t", [1, 1], F32)

    pa_mark = nc.sbuf_base

    rA = SB("rA", [1, T], F32)
    rB = SB("rB", [1, T], F32)
    rC = SB("rC", [1, T], F32)
    wosb = [SB("wosb0", [128, T], BF16)]
    c_f32 = [SB(f"c_f32{i}", [128, T], F32) for i in range(2)]
    c_bf = [SB(f"c_bf{i}", [128, T], BF16) for i in range(4)]
    xrot = [SB(f"xrot{i}", [128, T], F32) for i in range(2)]
    xb_scr = SB("xb0", [128, T], BF16)
    x2_scr = SB("x20", [128, T], BF16)
    xn_bf = SB("xn_bf", [128, NFC * T], BF16)
    winp = [SB(f"winp{i}", [128, NFC * 128], BF16) for i in range(2)]
    xp_bf = SB("xp_bf", [128, NEC * 520], BF16)
    cv0 = SB("cv0", [128, T], F32)
    cv1 = SB("cv1", [128, T], F32)
    u_bf = SB("u_bf", [128, NEC * T], BF16)
    du_scr = SB("du0", [128, T], BF16)
    siluz = SB("siluz", [128, NEC * T], BF16)
    delta = SB("delta", [128, NEC * T], F32)
    dtr_bf = SB("dtr_bf", [128, T], BF16)
    bc_bf = SB("bc_bf", [32, T], BF16)
    B_ball = [SB(f"Bball{h}", [128, HS * T], BF16) for h in range(NH)]
    C_ball = [SB(f"Cball{h}", [128, HS * T], BF16) for h in range(NH)]
    off_dA = nc.sbuf_base
    dA_b = SB("dA_b", [128, HS * T], F32)
    off_dbu = nc.sbuf_base
    dbu_b = SB("dbu_b", [128, HS * T], BF16)
    h_b = SB("h_b", [128, HS * T], BF16)
    hC_b2 = [SB(f"hC_b{i}", [128, HS * T], BF16) for i in range(2)]
    yhalf = [SB(f"yh{h}", [128, T], F32) for h in range(NH)]
    ybf = [SB(f"ybf{i}", [128, T], BF16) for i in range(2)]
    yscr = SB("yscr", [128, T], F32)
    wo_scr = SB("wo0", [128, T], F32)
    s1 = nc.alloc_sbuf_tensor_at("s1", [128, 4 * T], F32, offset=off_dbu)
    s2 = nc.alloc_sbuf_tensor_at("s2", [128, 2 * T], F32, offset=off_dA)
    pa_used = nc.sbuf_base

    PS = nc.alloc_psum_tensor
    pp = [PS(f"pp{i}", [128, T], F32) for i in range(2)]
    pp2 = PS("pp2", [128, T], F32)
    pb = [PS(f"pb{i}", [128, T], F32) for i in range(2)]
    po = [PS(f"po{i}", [128, T], F32) for i in range(2)]
    pstat = PS("pstat", [128, T], F32)

    t_xrot = [Trk() for _ in range(2)]
    t_xb = Trk()
    t_x2 = Trk()
    t_xn = [Trk() for _ in range(NFC)]
    t_winp = [Trk() for _ in range(2)]
    t_xp = [Trk() for _ in range(NEC)]
    t_cv0, t_cv1 = Trk(), Trk()
    t_u = [Trk() for _ in range(NEC)]
    t_du = Trk()
    t_siluz = [Trk() for _ in range(NEC)]
    t_delta = [Trk() for _ in range(NEC)]
    t_dtr = Trk()
    t_bc = Trk()
    t_B = [Trk() for _ in range(NH)]
    t_C = [Trk() for _ in range(NH)]
    t_dA = [Trk() for _ in range(HS)]
    t_dbu = Trk()
    t_hh = Trk()
    t_hC = [Trk(), Trk()]
    t_yh = [Trk() for _ in range(NH)]
    t_yscr = Trk()
    t_ybf = [Trk(), Trk()]
    t_wo = Trk()
    t_hist = [Trk() for _ in range(NEC)]
    t_carry = {(ec, h): Trk() for ec in range(NEC) for h in range(NH)}
    t_pp = [Trk() for _ in range(2)]
    t_pp2 = Trk()
    t_pb = [Trk() for _ in range(2)]
    t_po = [Trk() for _ in range(2)]
    t_pstat = Trk()
    t_rows = {k: Trk() for k in ("m", "e", "q", "r", "mr")}
    t_win_s = Trk()
    t_wout_s = Trk()
    t_w1_s = Trk()
    t_w2_s = Trk()
    t_sgu_s = Trk()
    t_sd_s = Trk()
    t_attn = [Trk() for _ in range(NCH)]
    t_stats = [Trk() for _ in range(NCH)]
    t_cf = [Trk(), Trk()]
    t_cb2 = [Trk() for _ in range(4)]
    strip_rot4 = [0]
    strip_rot = [0]
    t_const = Trk()
    t_Asb = Trk()
    t_wxp, t_wdt, t_rw, t_gw = Trk(), Trk(), Trk(), Trk()
    t_wosb = [Trk()]

    S.op("dve", lambda e: e.memset(ones_bf[:], 1.0), writes=(t_const,))
    S.op("dve", lambda e: e.memset(ones1c[:], 1.0), writes=(t_const,))
    for dst, src in ((ones1x, ones1x128_d),
                     (oh32, oh32_d), (oh8, oh8_d), (idn, idn_d),
                     (convw, convw_l), (convb, convb_l), (dtb, dtb_l),
                     (dskip, dskip_l)):
        S.dma("sp", lambda e, d=dst, s=src: e.dma_start(out=d[:], in_=s),
              writes=(t_const,), key="const")
    S.dma("sp", lambda e: e.dma_start(out=A_sb[:], in_=alog_l),
          writes=(t_Asb, t_const), key="const")
    S.op("act", lambda e: e.activation(out=A_sb[:], in_=A_sb[:], func=AF.Exp),
         reads=(t_Asb,), writes=(t_Asb,))
    S.op("dve", lambda e: e.tensor_scalar(out=A_sb[:], in0=A_sb[:],
         scalar1=-1.0, scalar2=None, op0=ALU.mult),
         reads=(t_Asb,), writes=(t_Asb,))
    S.op("dve", lambda e: e.memset(hcarry[:], 0.0),
         writes=tuple(t_carry.values()))
    S.op("pool", lambda e: e.memset(hist[:], 0.0), writes=tuple(t_hist))
    S.op("pool", lambda e: e.memset(xp_bf[:], 0.0), writes=tuple(t_xp))
    S.op("dve", lambda e: e.memset(eps_t[:], 1e-5), writes=(t_const,))

    def cast_small(dst_bf, src_dram, tr_dst, cols, parts=128):
        for lo in range(0, cols, T):
            i = strip_rot[0] = 1 - strip_rot[0]
            w = min(T, cols - lo)
            S.dma("sp", lambda e, s=src_dram, lo=lo, w=w, p=parts, i=i:
                  e.dma_start(
                out=c_f32[i][0:p, 0:w], in_=s[:, lo:lo+w]), writes=(t_cf[i],),
                key=f"cf{i}")
            S.op("act", lambda e, d=dst_bf, lo=lo, w=w, p=parts, i=i: e.copy(
                out=d[0:p, lo:lo+w], in_=c_f32[i][0:p, 0:w]),
                reads=(t_cf[i],), writes=(tr_dst,))
    cast_small(wxp_bf, wxp_l, t_wxp, NEC * 96)
    cast_small(rw_bf, rw_l, t_rw, NFC * E)
    cast_small(gw_bf, gw_l, t_gw, NFC)
    cast_small(wdt_bf, wdt_f, t_wdt, 1024)

    def cast_strip(src_ap, dst_ap, tr_dst, eng, skey):
        i = strip_rot[0] = 1 - strip_rot[0]
        jb = strip_rot4[0] = (strip_rot4[0] + 1) % 4
        S.dma("sp", lambda e, s=src_ap, i=i: e.dma_start(
            out=c_f32[i][:], in_=s), writes=(t_cf[i],), key=f"cf{i}")
        if eng == "act":
            S.op(eng, lambda e, i=i, jb=jb: e.copy(out=c_bf[jb][:],
                                                   in_=c_f32[i][:]),
                 reads=(t_cf[i],), writes=(t_cb2[jb],))
        else:
            S.op(eng, lambda e, i=i, jb=jb: e.tensor_copy(out=c_bf[jb][:],
                                                          in_=c_f32[i][:]),
                 reads=(t_cf[i],), writes=(t_cb2[jb],))
        S.dma("act", lambda e, d=dst_ap, jb=jb: e.dma_start(
            out=d, in_=c_bf[jb][:]),
            reads=(t_cb2[jb],), writes=(tr_dst,), key=skey)

    for r in ([] if _NOP0 else range(0, D, 128)):
        for cc in range(0, 2 * DI, T):
            cast_strip(w_in[r:r+128, cc:cc+T], win_s[r:r+128, cc:cc+T],
                       t_win_s, "dve", "swin")
    for ec in ([] if _NOP0 else range(NEC)):
        for j in range(2):
            cast_strip(w_out[ec*128:(ec+1)*128, j*T:(j+1)*T],
                       wout_s[ec*128:(ec+1)*128, j*T:(j+1)*T], t_wout_s,
                       "dve", "swout")

    ffn_strips = []
    for eidx in range(E):
        for r in range(0, D, 128):
            rr = eidx * D + r
            for cc in range(0, D, T):
                ffn_strips.append((w1f[rr:rr+128, cc:cc+T],
                                   w1_s[rr:rr+128, cc:cc+T], t_w1_s, "sw1"))
                ffn_strips.append((w2f[rr:rr+128, cc:cc+T],
                                   w2_s[rr:rr+128, cc:cc+T], t_w2_s, "sw2"))
    for r in range(0, D, 128):
        for cc in range(0, 2 * D, T):
            ffn_strips.append((sguf[r:r+128, cc:cc+T],
                               sgu_s[r:r+128, cc:cc+T], t_sgu_s, "ssgu"))
    for r in range(0, D, 128):
        for cc in range(0, D, T):
            ffn_strips.append((sdf[r:r+128, cc:cc+T],
                               sd_s[r:r+128, cc:cc+T], t_sd_s, "ssd"))
    strips_per = (len(ffn_strips) + NCH - 1) // NCH

    def emit_prep(c):
        c0 = c * T
        for fc in range(NFC):
            i = fc % 2
            S.dma("sp", lambda e, fc=fc, i=i: e.dma_start(
                out=xrot[i][:], in_=x_fm[fc*128:(fc+1)*128, c0:c0+T]),
                writes=(t_xrot[i],), key=f"xr{i}")
            S.op("dve", lambda e, i=i: e.tensor_copy(
                out=xb_scr[:], in_=xrot[i][:]),
                reads=(t_xrot[i],), writes=(t_xb,))
            S.op("act", lambda e, i=i: e.activation(
                out=x2_scr[:], in_=xrot[i][:], func=AF.Square),
                reads=(t_xrot[i],), writes=(t_x2,))
            S.op("pe", lambda e, fc=fc: e.matmul(
                pstat[0:1, :], ones1c[:], xb_scr[:],
                start=(fc == 0), stop=(fc == NFC - 1),
                skip_group_check=True),
                reads=(t_xb, t_const), writes=(t_pstat,))
            S.op("pe", lambda e, fc=fc: e.matmul(
                pb[0][0:1, :], ones1c[:], x2_scr[:],
                start=(fc == 0), stop=(fc == NFC - 1),
                skip_group_check=True),
                reads=(t_x2, t_const), writes=(t_pb[0],))
        S.op("dve", lambda e: e.tensor_scalar(out=rA[:], in0=pstat[0:1, :],
             scalar1=1.0 / D, scalar2=None, op0=ALU.mult),
             reads=(t_pstat,), writes=(t_rows["m"],))
        S.op("dve", lambda e: e.tensor_scalar(out=rB[:], in0=pb[0][0:1, :],
             scalar1=1.0 / D, scalar2=None, op0=ALU.mult),
             reads=(t_pb[0],), writes=(t_rows["e"],))
        S.op("dve", lambda e: e.tensor_tensor(out=rC[:], in0=rA[:],
             in1=rA[:], op=ALU.mult),
             reads=(t_rows["m"],), writes=(t_rows["q"],))
        S.op("dve", lambda e: e.tensor_tensor(out=rC[:], in0=rB[:],
             in1=rC[:], op=ALU.subtract),
             reads=(t_rows["e"], t_rows["q"]), writes=(t_rows["q"],))
        S.op("act", lambda e: e.activation(out=rC[:], in_=rC[:],
             func=AF.Sqrt, bias=eps_t[:]),
             reads=(t_rows["q"], t_const), writes=(t_rows["q"],))
        S.op("dve", lambda e: e.reciprocal(out=rB[:], in_=rC[:]),
             reads=(t_rows["q"], t_rows["e"]), writes=(t_rows["r"],))
        S.op("dve", lambda e: e.tensor_tensor(out=rC[:], in0=rA[:],
             in1=rB[:], op=ALU.mult),
             reads=(t_rows["m"], t_rows["r"], t_rows["q"]),
             writes=(t_rows["mr"],))
        S.dma("act", lambda e: e.dma_start(
            out=stats_s[0:1, c0:c0+T], in_=rB[:]),
            reads=(t_rows["r"],), writes=(t_stats[c],), key="statsR")
        S.dma("act", lambda e: e.dma_start(
            out=stats_s[1:2, c0:c0+T], in_=rC[:]),
            reads=(t_rows["mr"],), writes=(t_stats[c],), key="statsM")
        S.op("pe", lambda e: e.matmul(pb[0][:], ones1x[:], rB[:],
             start=True, stop=True),
             reads=(t_rows["r"], t_const), writes=(t_pb[0],))
        S.op("pe", lambda e: e.matmul(pb[1][:], ones1x[:], rC[:],
             start=True, stop=True),
             reads=(t_rows["mr"], t_const), writes=(t_pb[1],))
        for fc in range(NFC):
            i = fc % 2
            S.dma("sp", lambda e, fc=fc, i=i: e.dma_start(
                out=xrot[i][:], in_=x_fm[fc*128:(fc+1)*128, c0:c0+T]),
                writes=(t_xrot[i],), key=f"xr{i}")
            S.op("dve", lambda e, i=i: e.tensor_tensor(
                out=xrot[i][:], in0=xrot[i][:], in1=pb[0][:], op=ALU.mult),
                reads=(t_xrot[i], t_pb[0]), writes=(t_xrot[i],))
            S.op("dve", lambda e, i=i, fc=fc: e.tensor_tensor(
                out=xn_bf[:, fc*T:(fc+1)*T], in0=xrot[i][:], in1=pb[1][:],
                op=ALU.subtract),
                reads=(t_xrot[i], t_pb[1]), writes=(t_xn[fc],))
        for foc in range(32):
            ip = foc % 2
            j = foc % 2
            lo = foc * 128
            S.dma("sp", lambda e, ip=ip, lo=lo: e.dma_start(
                out=winp[ip][:].rearrange("q (fi f) -> q fi f", fi=NFC),
                in_=win_s[:, lo:lo+128].rearrange("(fi q) f -> q fi f", q=128)),
                reads=(t_win_s,), writes=(t_winp[ip],), key=f"winp{ip}")
            for fi in range(NFC):
                S.op("pe", lambda e, ip=ip, fi=fi, j=j: e.matmul(
                    pp[j][:],
                    winp[ip][:, fi*128:(fi+1)*128],
                    xn_bf[:, fi*T:(fi+1)*T],
                    start=(fi == 0), stop=(fi == NFC - 1)),
                    reads=(t_winp[ip], t_xn[fi]), writes=(t_pp[j],))
            if foc < NEC:
                S.op("act", lambda e, foc=foc, j=j: e.copy(
                    out=xp_bf[:, foc*520+3:foc*520+515], in_=pp[j][:]),
                    reads=(t_pp[j],), writes=(t_xp[foc],))
            else:
                ec = foc - NEC
                S.op("act", lambda e, ec=ec, j=j: e.activation(
                    out=siluz[:, ec*T:(ec+1)*T], in_=pp[j][:], func=AF.Silu),
                    reads=(t_pp[j],), writes=(t_siluz[ec],))
        for ec in range(NEC):
            xo = ec * 520
            S.op("pool", lambda e, ec=ec, xo=xo: e.tensor_copy(
                out=xp_bf[:, xo:xo+3], in_=hist[:, ec*3:(ec+1)*3]),
                reads=(t_hist[ec],), writes=(t_xp[ec],))
            S.op("dve", lambda e, ec=ec, xo=xo: e.tensor_scalar(
                out=cv0[:], in0=xp_bf[:, xo:xo+T],
                scalar1=convw[:, ec*4:ec*4+1], scalar2=None, op0=ALU.mult),
                reads=(t_xp[ec], t_const), writes=(t_cv0,))
            S.op("dve", lambda e, ec=ec, xo=xo: e.scalar_tensor_tensor(
                out=cv1[:], in0=xp_bf[:, xo+1:xo+1+T],
                scalar=convw[:, ec*4+1:ec*4+2], in1=cv0[:],
                op0=ALU.mult, op1=ALU.add),
                reads=(t_xp[ec], t_const, t_cv0), writes=(t_cv1,))
            S.op("dve", lambda e, ec=ec, xo=xo: e.scalar_tensor_tensor(
                out=cv0[:], in0=xp_bf[:, xo+2:xo+2+T],
                scalar=convw[:, ec*4+2:ec*4+3], in1=cv1[:],
                op0=ALU.mult, op1=ALU.add),
                reads=(t_xp[ec], t_const, t_cv1), writes=(t_cv0,))
            S.op("dve", lambda e, ec=ec, xo=xo: e.scalar_tensor_tensor(
                out=cv1[:], in0=xp_bf[:, xo+3:xo+3+T],
                scalar=convw[:, ec*4+3:ec*4+4], in1=cv0[:],
                op0=ALU.mult, op1=ALU.add),
                reads=(t_xp[ec], t_const, t_cv0), writes=(t_cv1,))
            S.op("pool", lambda e, ec=ec, xo=xo: e.tensor_copy(
                out=hist[:, ec*3:(ec+1)*3], in_=xp_bf[:, xo+T:xo+T+3]),
                reads=(t_xp[ec],), writes=(t_hist[ec],))
            S.op("act", lambda e, ec=ec: e.activation(
                out=u_bf[:, ec*T:(ec+1)*T], in_=cv1[:], func=AF.Silu,
                bias=convb[:, ec:ec+1]),
                reads=(t_cv1, t_const), writes=(t_u[ec],))
        for ec in range(NEC):
            S.op("pe", lambda e, ec=ec: e.matmul(
                pp2[0:96, :], wxp_bf[:, ec*96:(ec+1)*96],
                u_bf[:, ec*T:(ec+1)*T],
                start=(ec == 0), stop=(ec == NEC - 1)),
                reads=(t_wxp, t_u[ec]), writes=(t_pp2,))
        S.op("act", lambda e: e.copy(out=dtr_bf[0:64, :], in_=pp2[0:64, :]),
             reads=(t_pp2,), writes=(t_dtr,))
        S.op("act", lambda e: e.copy(out=dtr_bf[64:128, :], in_=pp2[0:64, :]),
             reads=(t_pp2,), writes=(t_dtr,))
        S.op("act", lambda e: e.copy(out=bc_bf[:], in_=pp2[64:96, :]),
             reads=(t_pp2,), writes=(t_bc,))
        for ec in range(NEC):
            j = ec % 2
            if ec < 8:
                wsl = wdt_bf[0:64, ec*128:(ec+1)*128]
                rsl = dtr_bf[0:64, :]
            else:
                wsl = wdt_bf[64:128, (ec-8)*128:(ec-7)*128]
                rsl = dtr_bf[64:128, :]
            S.op("pe", lambda e, wsl=wsl, rsl=rsl, j=j: e.matmul(
                pp[j][:], wsl, rsl,
                start=True, stop=True),
                reads=(t_wdt, t_dtr), writes=(t_pp[j],))
            S.op("act", lambda e, ec=ec, j=j: e.activation(
                out=cv0[:], in_=pp[j][:], func=AF.Relu, bias=dtb[:, ec:ec+1]),
                reads=(t_pp[j], t_const), writes=(t_cv0,))
            S.op("act", lambda e, ec=ec, j=j: e.activation(
                out=cv1[:], in_=pp[j][:], func=AF.Abs, bias=dtb[:, ec:ec+1]),
                reads=(t_pp[j], t_const), writes=(t_cv1,))
            S.op("act", lambda e: e.activation(
                out=cv1[:], in_=cv1[:], func=AF.Exp, scale=-1.0),
                reads=(t_cv1,), writes=(t_cv1,))
            S.op("act", lambda e: e.activation(
                out=cv1[:], in_=cv1[:], func=AF.Ln, bias=1.0),
                reads=(t_cv1,), writes=(t_cv1,))
            S.op("dve", lambda e, ec=ec: e.tensor_tensor(
                out=delta[:, ec*T:(ec+1)*T], in0=cv0[:], in1=cv1[:],
                op=ALU.add),
                reads=(t_cv0, t_cv1), writes=(t_delta[ec],))
        for h in range(NH):
            for n in range(HS):
                gn = h * HS + n
                j = n % 2
                S.op("pe", lambda e, gn=gn, j=j: e.matmul(
                    pb[j][:], oh32[:, gn:gn+1].broadcast_to([32, 128]),
                    bc_bf[:], start=True, stop=True),
                    reads=(t_bc, t_const), writes=(t_pb[j],))
                S.op("act", lambda e, h=h, n=n, j=j: e.copy(
                    out=B_ball[h][:, n*T:(n+1)*T], in_=pb[j][:]),
                    reads=(t_pb[j],), writes=(t_B[h],))
                S.op("pe", lambda e, gn=gn, j=j: e.matmul(
                    pb[j][:], oh32[:, 16+gn:16+gn+1].broadcast_to([32, 128]),
                    bc_bf[:], start=True, stop=True),
                    reads=(t_bc, t_const), writes=(t_pb[j],))
                S.op("act", lambda e, h=h, n=n, j=j: e.copy(
                    out=C_ball[h][:, n*T:(n+1)*T], in_=pb[j][:]),
                    reads=(t_pb[j],), writes=(t_C[h],))

    def emit_scan(c):
        c0 = c * T
        banks = [pp[0], pp[1], pp2, pb[0], pb[1], po[0], po[1], pstat]
        tbanks = [t_pp[0], t_pp[1], t_pp2, t_pb[0], t_pb[1], t_po[0],
                  t_po[1], t_pstat]
        pend = []

        def emit_tree(it, ec, h):
            ib = it % 2
            S.op("dve", lambda e, ib=ib: e.tensor_tensor(
                out=s1[:], in0=hC_b2[ib][:, 0:4*T], in1=hC_b2[ib][:, 4*T:8*T],
                op=ALU.add),
                reads=(t_hC[ib],), writes=(t_dbu,))
            S.op("dve", lambda e: e.tensor_tensor(
                out=s2[:], in0=s1[:, 0:2*T], in1=s1[:, 2*T:4*T],
                op=ALU.add),
                reads=(t_dbu,), writes=(t_dA[0], t_dA[1])),
            S.op("dve", lambda e, h=h: e.tensor_tensor(
                out=yhalf[h][:], in0=s2[:, 0:T], in1=s2[:, T:2*T],
                op=ALU.add),
                reads=(t_dA[0], t_dA[1]), writes=(t_yh[h],))
            if h == NH - 1:
                yb = (it // 2) % 2
                S.op("dve", lambda e: e.tensor_tensor(
                    out=yscr[:], in0=yhalf[0][:], in1=yhalf[1][:], op=ALU.add),
                    reads=(t_yh[0], t_yh[1]), writes=(t_yscr,))
                S.op("dve", lambda e, ec=ec: e.scalar_tensor_tensor(
                    out=yscr[:], in0=u_bf[:, ec*T:(ec+1)*T],
                    scalar=dskip[:, ec:ec+1], in1=yscr[:],
                    op0=ALU.mult, op1=ALU.add),
                    reads=(t_u[ec], t_const, t_yscr), writes=(t_yscr,))
                S.op("dve", lambda e, ec=ec, yb=yb: e.tensor_tensor(
                    out=ybf[yb][:], in0=yscr[:],
                    in1=siluz[:, ec*T:(ec+1)*T], op=ALU.mult),
                    reads=(t_yscr, t_siluz[ec]), writes=(t_ybf[yb],))
                for hw in range(2):
                    ie = 0
                    S.dma("sp", lambda e, ec=ec, hw=hw: e.dma_start(
                        out=wosb[0][:],
                        in_=wout_s[ec*128:(ec+1)*128, hw*T:(hw+1)*T]),
                        reads=(t_wout_s,), writes=(t_wosb[0],), key="wo0")
                    for dq in range(4):
                        dc = hw * 4 + dq
                        S.op("pe", lambda e, ec=ec, ie=ie, dq=dq, dc=dc,
                             yb=yb: e.matmul(
                            banks[dc][:], wosb[ie][:, dq*128:(dq+1)*128],
                            ybf[yb][:],
                            start=(ec == 0), stop=(ec == NEC - 1),
                            skip_group_check=True),
                            reads=(t_wosb[ie], t_ybf[yb]),
                            writes=(tbanks[dc],))

        it = 0
        for ec in range(NEC):
            S.op("dve", lambda e, ec=ec: e.tensor_tensor(
                out=du_scr[:], in0=delta[:, ec*T:(ec+1)*T],
                in1=u_bf[:, ec*T:(ec+1)*T], op=ALU.mult),
                reads=(t_delta[ec], t_u[ec]), writes=(t_du,))
            for h in range(NH):
                ib = it % 2
                for n in range(HS):
                    gn = h * HS + n
                    S.op("act", lambda e, ec=ec, n=n, gn=gn: e.activation(
                        out=dA_b[:, n*T:(n+1)*T],
                        in_=delta[:, ec*T:(ec+1)*T], func=AF.Exp,
                        scale=A_sb[:, ec*DS+gn:ec*DS+gn+1]),
                        reads=(t_delta[ec], t_Asb), writes=(t_dA[n],))
                S.op("dve", lambda e, h=h: e.tensor_tensor(
                    out=dbu_b[:].rearrange("p (n t) -> p n t", n=HS),
                    in0=du_scr[:, None, :].broadcast_to([128, HS, T]),
                    in1=B_ball[h][:].rearrange("p (n t) -> p n t", n=HS),
                    op=ALU.mult),
                    reads=(t_du, t_B[h]), writes=(t_dbu,))
                co = ec * DS + h * HS
                for n in range(HS):
                    S.op("dve", lambda e, n=n, co=co: e.tensor_tensor_scan(
                        out=h_b[:, n*T:(n+1)*T], data0=dA_b[:, n*T:(n+1)*T],
                        data1=dbu_b[:, n*T:(n+1)*T],
                        initial=hcarry[:, co+n:co+n+1],
                        op0=ALU.mult, op1=ALU.add),
                        reads=(t_dA[n], t_dbu, t_carry[(ec, h)]),
                        writes=(t_hh,))
                hcv = h_b[:].rearrange("p (n t) -> p n t", n=HS)[:, :, T-1]
                S.op("pool", lambda e, hcv=hcv, co=co: e.tensor_copy(
                    out=hcarry[:, co:co+HS], in_=hcv),
                    reads=(t_hh,), writes=(t_carry[(ec, h)],))
                S.op("pool", lambda e, h=h, ib=ib: e.tensor_tensor(
                    out=hC_b2[ib][:], in0=h_b[:], in1=C_ball[h][:],
                    op=ALU.mult),
                    reads=(t_hh, t_C[h]), writes=(t_hC[ib],))
                pend.append((it, ec, h))
                if len(pend) > 1:
                    emit_tree(*pend.pop(0))
                it += 1
        while pend:
            emit_tree(*pend.pop(0))
        for dc in range(NFC):
            S.op("act", lambda e, dc=dc: e.copy(
                out=wo_scr[:], in_=banks[dc][:]),
                reads=(tbanks[dc],), writes=(t_wo,))
            S.dma("act", lambda e, dc=dc, c0=c0: e.dma_start(
                out=attn_s[dc*128:(dc+1)*128, c0:c0+T], in_=wo_scr[:]),
                reads=(t_wo,), writes=(t_attn[c],), key="attn")

    strip_i = 0
    for c in range(NCH + 1):
        if c < NCH:
            emit_prep(c)
            emit_scan(c)
            hi = min(len(ffn_strips), strip_i + strips_per)
            for k in range(strip_i, hi):
                srcx, dstx, tr, skey = ffn_strips[k]
                cast_strip(srcx, dstx, tr, ("dve", "act")[k % 2], skey)
            strip_i = hi

    pa_done = []
    for eng in ENGINES:
        if S.ecount[eng]:
            pa_done.append(Evt(S.esem[eng], S.ecount[eng]))
    for k, cnt in S.qcount.items():
        if cnt:
            pa_done.append(Evt(S.qsem[k], cnt))

    pb_ptr = [((pa_mark + 63) // 64) * 64]

    def PBA(name, shape, dt):
        nbytes = int(np.prod(shape[1:])) * (2 if dt == BF16 else 4)
        off = pb_ptr[0]
        hnd = nc.alloc_sbuf_tensor_at(name, shape, dt, offset=off)
        pb_ptr[0] = ((off + nbytes + 63) // 64) * 64
        return hnd

    xchB = PBA("xchB", [128, NFC * T], F32)
    xnB = PBA("xnB", [128, NFC * T], BF16)
    gsB = PBA("gsB", [128, NFC * T], BF16)
    vsc = PBA("vsc", [128, T], F32)
    vgB = PBA("vgB", [128, NFC * T], BF16)
    hB = PBA("hB", [128, NFC * T], BF16)
    hsB = PBA("hsB", [128, NFC * T], BF16)
    oacc = PBA("oacc", [128, NFC * T], F32)
    lg_sb = PBA("lg_sb", [8, T], F32)
    lgT = PBA("lgT", [128, 32], F32)
    m1s = PBA("m1s", [128, 4], F32)
    nm1 = PBA("nm1", [128, 4], F32)
    ssum = PBA("ssum", [128, 4], F32)
    rsum = PBA("rsum", [128, 4], F32)
    m2s = PBA("m2s", [128, 4], F32)
    pexp = PBA("pexp", [128, 32], F32)
    mk1 = PBA("mk1", [128, 32], F32)
    lg2 = PBA("lg2", [128, 32], F32)
    mk2 = PBA("mk2", [128, 32], F32)
    combT = PBA("combT", [128, 32], F32)
    comb_fm = PBA("comb_fm", [8, T], F32)
    g_row = PBA("g_row", [1, T], F32)
    rB_row = PBA("rB_row", [1, T], F32)
    mrB_row = PBA("mrB_row", [1, T], F32)
    w1p = [PBA(f"w1p{i}", [128, NFC * D], BF16) for i in range(2)]
    w2p = [PBA(f"w2p{i}", [128, NFC * D], BF16) for i in range(2)]
    sguB = PBA("sguB", [128, NFC * 2 * D], BF16)
    sdB = PBA("sdB", [128, NFC * D], BF16)
    assert pb_ptr[0] <= nc.SBUF_PARTITION_SIZE_BYTES, pb_ptr[0]
    assert pb_ptr[0] <= pa_used + 64, (pb_ptr[0], pa_used)

    def TB():
        return Trk(pa_done)
    t_xchB = TB()
    t_xnB = [TB() for _ in range(NFC)]
    t_gsB = [TB() for _ in range(NFC)]
    t_vsc = TB()
    t_vgB = [TB() for _ in range(NFC)]
    t_hB = [TB() for _ in range(NFC)]
    t_hsB = [TB() for _ in range(NFC)]
    t_oacc = [TB() for _ in range(NFC)]
    t_lg = TB()
    t_lgT = TB()
    t_small = TB()
    t_combT = TB()
    t_cfm = TB()
    t_grow = TB()
    t_w1p = [TB() for _ in range(2)]
    t_w2p = [TB() for _ in range(2)]
    t_sguB = TB()
    t_sdB = TB()
    t_rB = TB()
    t_mrB = TB()
    for t in (t_pp[0], t_pp[1], t_pp2, t_pb[0], t_pb[1], t_po[0], t_po[1],
              t_pstat):
        t.w = t.w + pa_done

    out_store_evts = []

    def emit_phaseB(c):
        c0 = c * T
        for fc in range(NFC):
            S.dma("sp", lambda e, fc=fc: e.dma_start(
                out=xchB[:, fc*T:(fc+1)*T],
                in_=x_fm[fc*128:(fc+1)*128, c0:c0+T]),
                writes=(t_xchB,), key="xB")
        S.dma("sp", lambda e: e.dma_start(
            out=rB_row[:], in_=stats_s[0:1, c0:c0+T]),
            reads=(t_stats[c],), writes=(t_rB,), key="strB")
        S.dma("sp", lambda e: e.dma_start(
            out=mrB_row[:], in_=stats_s[1:2, c0:c0+T]),
            reads=(t_stats[c],), writes=(t_mrB,), key="strM")
        S.op("pe", lambda e: e.matmul(pb[0][:], ones1x[:], rB_row[:],
             start=True, stop=True),
             reads=(t_rB, t_const), writes=(t_pb[0],))
        S.op("pe", lambda e: e.matmul(pb[1][:], ones1x[:], mrB_row[:],
             start=True, stop=True),
             reads=(t_mrB, t_const), writes=(t_pb[1],))
        for fc in range(NFC):
            S.op("dve", lambda e, fc=fc: e.tensor_tensor(
                out=vsc[:], in0=xchB[:, fc*T:(fc+1)*T], in1=pb[0][:],
                op=ALU.mult),
                reads=(t_xchB, t_pb[0]), writes=(t_vsc,))
            S.op("dve", lambda e, fc=fc: e.tensor_tensor(
                out=xnB[:, fc*T:(fc+1)*T], in0=vsc[:], in1=pb[1][:],
                op=ALU.subtract),
                reads=(t_vsc, t_pb[1]), writes=(t_xnB[fc],))
        oacc_evts = []
        for fc in range(NFC):
            ev = S.dma("sp", lambda e, fc=fc: e.dma_start(
                out=oacc[:, fc*T:(fc+1)*T],
                in_=attn_s[fc*128:(fc+1)*128, c0:c0+T]),
                reads=(t_attn[c],), writes=(t_oacc[fc],), key="oaccl")
            oacc_evts.append(ev)
        for fc in range(NFC):
            t_oacc[fc].w = [oacc_evts[-1]]
        for fi in range(NFC):
            S.op("pe", lambda e, fi=fi: e.matmul(
                pp2[0:E, :], rw_bf[:, fi*E:(fi+1)*E], xnB[:, fi*T:(fi+1)*T],
                start=(fi == 0), stop=(fi == NFC - 1)),
                reads=(t_rw, t_xnB[fi]), writes=(t_pp2,))
        S.op("act", lambda e: e.copy(out=lg_sb[:], in_=pp2[0:E, :]),
             reads=(t_pp2,), writes=(t_lg,))
        for jj in range(4):
            S.op("pe", lambda e, jj=jj: e.transpose(
                pb[0][:, jj*8:(jj+1)*8], lg_sb[:, jj*128:(jj+1)*128],
                idn[0:8, 0:8]),
                reads=(t_lg, t_const), writes=(t_pb[0],))
        S.op("dve", lambda e: e.tensor_copy(out=lgT[:], in_=pb[0][:, 0:32]),
             reads=(t_pb[0],), writes=(t_lgT,))
        SL = [slice(jj*8, (jj+1)*8) for jj in range(4)]
        for jj in range(4):
            S.op("dve", lambda e, jj=jj: e.reduce_max(
                out=m1s[:, jj:jj+1], in_=lgT[:, SL[jj]], axis=AX.X),
                reads=(t_lgT,), writes=(t_small,))
        for jj in range(4):
            S.op("dve", lambda e, jj=jj: e.tensor_scalar(
                out=nm1[:, jj:jj+1], in0=m1s[:, jj:jj+1], scalar1=-1.0,
                scalar2=None, op0=ALU.mult),
                reads=(t_small,), writes=(t_small,))
        for jj in range(4):
            S.op("act", lambda e, jj=jj: e.activation(
                out=pexp[:, SL[jj]], in_=lgT[:, SL[jj]], func=AF.Exp,
                bias=nm1[:, jj:jj+1]),
                reads=(t_lgT, t_small), writes=(t_small,))
        for jj in range(4):
            S.op("dve", lambda e, jj=jj: e.reduce_sum(
                out=ssum[:, jj:jj+1], in_=pexp[:, SL[jj]], axis=AX.X),
                reads=(t_small,), writes=(t_small,))
        for jj in range(4):
            S.op("dve", lambda e, jj=jj: e.reciprocal(
                out=rsum[:, jj:jj+1], in_=ssum[:, jj:jj+1]),
                reads=(t_small,), writes=(t_small,))
        for jj in range(4):
            S.op("dve", lambda e, jj=jj: e.tensor_scalar(
                out=mk1[:, SL[jj]], in0=lgT[:, SL[jj]],
                scalar1=m1s[:, jj:jj+1], scalar2=None, op0=ALU.is_equal),
                reads=(t_lgT, t_small), writes=(t_small,))
        for jj in range(4):
            S.op("dve", lambda e, jj=jj: e.tensor_scalar(
                out=lg2[:, SL[jj]], in0=mk1[:, SL[jj]], scalar1=-1e9,
                scalar2=None, op0=ALU.mult),
                reads=(t_small,), writes=(t_small,))
        for jj in range(4):
            S.op("dve", lambda e, jj=jj: e.tensor_tensor(
                out=lg2[:, SL[jj]], in0=lgT[:, SL[jj]], in1=lg2[:, SL[jj]],
                op=ALU.add),
                reads=(t_lgT, t_small), writes=(t_small,))
        for jj in range(4):
            S.op("dve", lambda e, jj=jj: e.reduce_max(
                out=m2s[:, jj:jj+1], in_=lg2[:, SL[jj]], axis=AX.X),
                reads=(t_small,), writes=(t_small,))
        for jj in range(4):
            S.op("dve", lambda e, jj=jj: e.tensor_scalar(
                out=mk2[:, SL[jj]], in0=lg2[:, SL[jj]],
                scalar1=m2s[:, jj:jj+1], scalar2=None, op0=ALU.is_equal),
                reads=(t_small,), writes=(t_small,))
        for jj in range(4):
            S.op("dve", lambda e, jj=jj: e.tensor_tensor(
                out=mk1[:, SL[jj]], in0=mk1[:, SL[jj]], in1=mk2[:, SL[jj]],
                op=ALU.add),
                reads=(t_small,), writes=(t_small,))
        for jj in range(4):
            S.op("dve", lambda e, jj=jj: e.tensor_tensor(
                out=pexp[:, SL[jj]], in0=pexp[:, SL[jj]], in1=mk1[:, SL[jj]],
                op=ALU.mult),
                reads=(t_small,), writes=(t_small,))
        for jj in range(4):
            S.op("dve", lambda e, jj=jj: e.tensor_scalar(
                out=combT[:, SL[jj]], in0=pexp[:, SL[jj]],
                scalar1=rsum[:, jj:jj+1], scalar2=None, op0=ALU.mult),
                reads=(t_small,), writes=(t_combT,))
        for jj in range(4):
            S.op("pe", lambda e, jj=jj: e.transpose(
                pp2[0:8, jj*128:(jj+1)*128], combT[:, jj*8:(jj+1)*8], idn[:]),
                reads=(t_combT, t_const), writes=(t_pp2,))
        S.op("act", lambda e: e.copy(out=comb_fm[:], in_=pp2[0:8, :]),
             reads=(t_pp2,), writes=(t_cfm,))
        for fi in range(NFC):
            S.op("pe", lambda e, fi=fi: e.matmul(
                pstat[0:1, :], gw_bf[:, fi:fi+1], xnB[:, fi*T:(fi+1)*T],
                start=(fi == 0), stop=(fi == NFC - 1)),
                reads=(t_gw, t_xnB[fi]), writes=(t_pstat,))
        S.op("act", lambda e: e.activation(out=g_row[:], in_=pstat[0:1, :],
             func=AF.Sigmoid),
             reads=(t_pstat,), writes=(t_grow,))
        S.op("pe", lambda e: e.matmul(pb[0][:], ones1x[:], g_row[:],
             start=True, stop=True),
             reads=(t_grow, t_const), writes=(t_pb[0],))
        S.dma("sp", lambda e: e.dma_start(
            out=sguB[:].rearrange("q (fi f) -> q fi f", fi=NFC),
            in_=sgu_s[:, :].rearrange("(fi q) f -> q fi f", q=128)),
            reads=(t_sgu_s,), writes=(t_sguB,), key="sguL")
        for foc in range(16):
            j = foc % 2
            for fi in range(NFC):
                S.op("pe", lambda e, foc=foc, fi=fi, j=j: e.matmul(
                    pp[j][:], sguB[:, fi*2*D + foc*128: fi*2*D + foc*128 + 128],
                    xnB[:, fi*T:(fi+1)*T],
                    start=(fi == 0), stop=(fi == NFC - 1)),
                    reads=(t_sguB, t_xnB[fi]), writes=(t_pp[j],))
            if foc < NFC:
                S.op("act", lambda e, foc=foc, j=j: e.activation(
                    out=gsB[:, foc*T:(foc+1)*T], in_=pp[j][:], func=AF.Silu),
                    reads=(t_pp[j],), writes=(t_gsB[foc],))
            else:
                uc = foc - NFC
                S.op("dve", lambda e, uc=uc, j=j: e.tensor_tensor(
                    out=vsc[:], in0=gsB[:, uc*T:(uc+1)*T], in1=pp[j][:],
                    op=ALU.mult),
                    reads=(t_gsB[uc], t_pp[j]), writes=(t_vsc,))
                S.op("dve", lambda e, uc=uc: e.tensor_tensor(
                    out=vgB[:, uc*T:(uc+1)*T], in0=vsc[:], in1=pb[0][:],
                    op=ALU.mult),
                    reads=(t_vsc, t_pb[0]), writes=(t_vgB[uc],))
        S.dma("sp", lambda e: e.dma_start(
            out=sdB[:].rearrange("q (fi f) -> q fi f", fi=NFC),
            in_=sd_s[:, :].rearrange("(fi q) f -> q fi f", q=128)),
            reads=(t_sd_s,), writes=(t_sdB,), key="sdL")
        for dc in range(NFC):
            j = dc % 2
            for fi in range(NFC):
                S.op("pe", lambda e, dc=dc, fi=fi, j=j: e.matmul(
                    po[j][:], sdB[:, fi*D + dc*128: fi*D + dc*128 + 128],
                    vgB[:, fi*T:(fi+1)*T],
                    start=(fi == 0), stop=(fi == NFC - 1)),
                    reads=(t_sdB, t_vgB[fi]), writes=(t_po[j],))
            S.op("dve", lambda e, dc=dc, j=j: e.tensor_tensor(
                out=oacc[:, dc*T:(dc+1)*T], in0=oacc[:, dc*T:(dc+1)*T],
                in1=po[j][:], op=ALU.add),
                reads=(t_po[j], t_oacc[dc]), writes=(t_oacc[dc],))
        for eidx in range(E):
            ie = eidx % 2
            S.dma("sp", lambda e, ie=ie, eidx=eidx: e.dma_start(
                out=w1p[ie][:].rearrange("q (fi f) -> q fi f", fi=NFC),
                in_=w1_s[eidx*D:(eidx+1)*D, :].rearrange(
                    "(fi q) f -> q fi f", q=128)),
                reads=(t_w1_s,), writes=(t_w1p[ie],), key=f"w1L{ie}")
            S.dma("sp", lambda e, ie=ie, eidx=eidx: e.dma_start(
                out=w2p[ie][:].rearrange("q (fi f) -> q fi f", fi=NFC),
                in_=w2_s[eidx*D:(eidx+1)*D, :].rearrange(
                    "(fi q) f -> q fi f", q=128)),
                reads=(t_w2_s,), writes=(t_w2p[ie],), key=f"w2L{ie}")
            for foc in range(NFC):
                j = foc % 2
                for fi in range(NFC):
                    S.op("pe", lambda e, ie=ie, foc=foc, fi=fi, j=j: e.matmul(
                        pp[j][:],
                        w1p[ie][:, fi*D + foc*128: fi*D + foc*128 + 128],
                        xnB[:, fi*T:(fi+1)*T],
                        start=(fi == 0), stop=(fi == NFC - 1)),
                        reads=(t_w1p[ie], t_xnB[fi]), writes=(t_pp[j],))
                S.op("act", lambda e, foc=foc, j=j: e.activation(
                    out=hB[:, foc*T:(foc+1)*T], in_=pp[j][:], func=AF.Silu),
                    reads=(t_pp[j],), writes=(t_hB[foc],))
            S.op("pe", lambda e, eidx=eidx: e.matmul(
                pb[1][:], oh8[:, eidx:eidx+1].broadcast_to([8, 128]),
                comb_fm[:], start=True, stop=True),
                reads=(t_cfm, t_const), writes=(t_pb[1],))
            for foc in range(NFC):
                S.op("dve", lambda e, foc=foc: e.tensor_tensor(
                    out=hsB[:, foc*T:(foc+1)*T], in0=hB[:, foc*T:(foc+1)*T],
                    in1=pb[1][:], op=ALU.mult),
                    reads=(t_hB[foc], t_pb[1]), writes=(t_hsB[foc],))
            for dc in range(NFC):
                j = dc % 2
                for fi in range(NFC):
                    S.op("pe", lambda e, ie=ie, dc=dc, fi=fi, j=j: e.matmul(
                        po[j][:],
                        w2p[ie][:, fi*D + dc*128: fi*D + dc*128 + 128],
                        hsB[:, fi*T:(fi+1)*T],
                        start=(fi == 0), stop=(fi == NFC - 1)),
                        reads=(t_w2p[ie], t_hsB[fi]), writes=(t_po[j],))
                S.op("dve", lambda e, dc=dc, j=j: e.tensor_tensor(
                    out=oacc[:, dc*T:(dc+1)*T], in0=oacc[:, dc*T:(dc+1)*T],
                    in1=po[j][:], op=ALU.add),
                    reads=(t_po[j], t_oacc[dc]), writes=(t_oacc[dc],))
        st_evts = []
        for fc in range(NFC):
            S.op("dve", lambda e, fc=fc: e.tensor_tensor(
                out=oacc[:, fc*T:(fc+1)*T], in0=oacc[:, fc*T:(fc+1)*T],
                in1=xchB[:, fc*T:(fc+1)*T], op=ALU.add),
                reads=(t_xchB, t_oacc[fc]), writes=(t_oacc[fc],))
            evt = S.dma("act", lambda e, fc=fc: e.dma_start(
                out=out_fm[fc*128:(fc+1)*128, c0:c0+T],
                in_=oacc[:, fc*T:(fc+1)*T]),
                reads=(t_oacc[fc],), key="outst")
            st_evts.append(evt)
            out_store_evts.append(evt)
        for fc in range(NFC):
            t_oacc[fc].r["q_outst"] = st_evts[-1]

    if not _LNONLY:
        for c in range(NCH):
            emit_phaseB(c)

    last = out_store_evts[-1]
    S.raw("sp", lambda e, lv=last: e.wait_ge(lv.sem, lv.val))

    with nc.Block() as block:
        S.replay(block)
    return nc


_NC_CACHE = {}


def kernel(L0_attn_x, L0_attn_W_in, L0_attn_conv_w, L0_attn_conv_b,
           L0_attn_W_xp, L0_attn_W_dt, L0_attn_dt_bias, L0_attn_A_log,
           L0_attn_D_skip, L0_attn_W_out, L0_ffn_router_w, L0_ffn_w1,
           L0_ffn_w2, L0_ffn_shared_gate_up, L0_ffn_shared_down,
           L0_ffn_shared_gate_w):
    f32 = np.float32
    x = np.asarray(L0_attn_x, f32)
    ncores = x.shape[0]

    def lay_pc(v, nec):
        return np.ascontiguousarray(np.asarray(v, f32).reshape(nec, 128).T)

    w_in = np.ascontiguousarray(np.asarray(L0_attn_W_in, f32))
    w_out = np.ascontiguousarray(np.asarray(L0_attn_W_out, f32))
    w1f = np.ascontiguousarray(np.asarray(L0_ffn_w1, f32).reshape(E * D, D))
    w2f = np.ascontiguousarray(np.asarray(L0_ffn_w2, f32).reshape(E * D, D))
    sguf = np.ascontiguousarray(np.asarray(L0_ffn_shared_gate_up, f32))
    sdf = np.ascontiguousarray(np.asarray(L0_ffn_shared_down, f32))
    wxp_l = np.ascontiguousarray(
        np.asarray(L0_attn_W_xp, f32).reshape(NEC, 128, 96)
        .transpose(1, 0, 2).reshape(128, NEC * 96))
    wdt0 = np.asarray(L0_attn_W_dt, f32)
    wdt_f = np.ascontiguousarray(
        np.concatenate([wdt0[:, 0:1024], wdt0[:, 1024:2048]], axis=0))
    rw_l = np.ascontiguousarray(
        np.asarray(L0_ffn_router_w, f32).reshape(NFC, 128, E)
        .transpose(1, 0, 2).reshape(128, NFC * E))
    gw_l = np.ascontiguousarray(
        np.asarray(L0_ffn_shared_gate_w, f32).reshape(NFC, 128).T)
    convw_l = np.ascontiguousarray(
        np.asarray(L0_attn_conv_w, f32).reshape(NEC, 128, 4)
        .transpose(1, 0, 2).reshape(128, NEC * 4))
    convb_l = lay_pc(L0_attn_conv_b, NEC)
    dtb_l = lay_pc(L0_attn_dt_bias, NEC)
    alog_l = np.ascontiguousarray(
        np.asarray(L0_attn_A_log, f32).reshape(NEC, 128, DS)
        .transpose(1, 0, 2).reshape(128, NEC * DS))
    dskip_l = lay_pc(L0_attn_D_skip, NEC)

    bf = ml_dtypes.bfloat16
    ones_bf_d = np.ones((128, 2), dtype=bf)
    ones1x128_d = np.ones((1, 128), dtype=f32)
    oh32 = np.eye(32, dtype=bf)
    oh8 = np.eye(8, dtype=f32)
    idn_d = np.eye(128, dtype=f32)

    key = "p"
    if key not in _NC_CACHE:
        _NC_CACHE[key] = build_program()
    nc = _NC_CACHE[key]

    shared = dict(
        w_in=w_in, w_out=w_out, w1f=w1f, w2f=w2f, sguf=sguf, sdf=sdf,
        wxp_l=wxp_l, wdt_f=wdt_f, rw_l=rw_l, gw_l=gw_l, convw_l=convw_l,
        convb_l=convb_l, dtb_l=dtb_l, alog_l=alog_l, dskip_l=dskip_l,
        ones_bf_d=ones_bf_d, ones1x128_d=ones1x128_d, oh32_d=oh32,
        oh8_d=oh8, idn_d=idn_d)
    in_maps = []
    for i in range(ncores):
        m = dict(shared)
        m["x_fm"] = np.ascontiguousarray(x[i].T)
        in_maps.append(m)

    res = run_bass_kernel_spmd(nc, in_maps, core_ids=list(range(ncores)))
    out = np.stack([np.ascontiguousarray(r["out_fm"].T)
                    for r in res.results], axis=0)
    return out.astype(np.float32)
